# version 36
# baseline (speedup 1.0000x reference)
import numpy as np
from contextlib import ExitStack
from collections import deque

import concourse.bass as bass
import concourse.mybir as mybir
from concourse.bass_utils import run_bass_kernel_spmd

F32 = mybir.dt.float32
BF16 = mybir.dt.bfloat16
Alu = mybir.AluOpType
Act = mybir.ActivationFunctionType

S = 4096
DM = 1024
NB = 2
T = 512
NT = S // T
DC = DM // 128
EPS = 1e-5
SHIFT_TILES = 4


class Sched:
    def __init__(self):
        self.steps = []
        self._writers = {}
        self._readers = {}

    def add(self, eng, emit, waits=(), inc=None, n=1, amt=1, key=None,
            reads=(), writes=()):
        w = list(waits)
        for r in writes:
            w += self._readers.get(r, [])
            if r in self._writers:
                w.append(self._writers[r])
        for r in reads:
            if r in self._writers:
                w.append(self._writers[r])
        self.steps.append(dict(eng=eng, emit=emit, waits=tuple(dict.fromkeys(w)),
                               inc=inc, n=n, amt=amt, key=key))
        if key is not None:
            for r in writes:
                self._writers[r] = key
                self._readers[r] = []
            for r in reads:
                self._readers.setdefault(r, []).append(key)

    def finalize(self, nc, es):
        sem_names = sorted({st["inc"] for st in self.steps if st["inc"]})
        self.handles = {s: es.enter_context(nc.semaphore(s)) for s in sem_names}
        cnt = {s: 0 for s in sem_names}
        marks = {}
        for st in self.steps:
            if st["inc"]:
                cnt[st["inc"]] += st["n"] * st["amt"]
                if st["key"]:
                    marks[st["key"]] = (st["inc"], cnt[st["inc"]])
        self.marks = marks

    def emit_engine(self, engname):
        marks, handles = self.marks, self.handles

        def body(eng):
            hw = {}
            for st in self.steps:
                if st["eng"] != engname:
                    continue
                for w in st["waits"]:
                    if w not in marks:
                        continue
                    sname, v = marks[w]
                    if hw.get(sname, 0) >= v:
                        continue
                    eng.wait_ge(handles[sname], v)
                    hw[sname] = v
                insts = st["emit"](eng)
                if st["inc"]:
                    if insts is None:
                        insts = []
                    if not isinstance(insts, (list, tuple)):
                        insts = [insts]
                    assert len(insts) == st["n"], (
                        f"step {st['key']}: expected {st['n']} incs, got {len(insts)}")
                    for i in insts:
                        i.then_inc(handles[st["inc"]], st["amt"])
        return body


def build_nc():
    nc = bass.Bass("TRN2", target_bir_lowering=False, detect_race_conditions=False)

    x_fm = nc.dram_tensor("x_fm", [NB, DM, S], F32, kind="ExternalInput").ap()
    y_fm = nc.dram_tensor("y_fm", [NB, DM, S], F32, kind="ExternalOutput").ap()
    w1 = nc.dram_tensor("w1", [DM, DM], F32, kind="ExternalInput").ap()
    w2 = nc.dram_tensor("w2", [DM, 512], F32, kind="ExternalInput").ap()
    w3 = nc.dram_tensor("w3", [512, 256], F32, kind="ExternalInput").ap()
    w4 = nc.dram_tensor("w4", [256, 1], F32, kind="ExternalInput").ap()
    b1 = nc.dram_tensor("b1", [DM], F32, kind="ExternalInput").ap()
    b2 = nc.dram_tensor("b2", [512], F32, kind="ExternalInput").ap()
    b3 = nc.dram_tensor("b3", [256], F32, kind="ExternalInput").ap()
    b4 = nc.dram_tensor("b4", [1], F32, kind="ExternalInput").ap()
    gw1 = nc.dram_tensor("gw1", [DM, 512], F32, kind="ExternalInput").ap()
    gw2 = nc.dram_tensor("gw2", [512, 256], F32, kind="ExternalInput").ap()
    gw3 = nc.dram_tensor("gw3", [256, 1], F32, kind="ExternalInput").ap()
    gb1 = nc.dram_tensor("gb1", [512], F32, kind="ExternalInput").ap()
    gb2 = nc.dram_tensor("gb2", [256], F32, kind="ExternalInput").ap()
    gb3 = nc.dram_tensor("gb3", [1], F32, kind="ExternalInput").ap()
    arr = nc.dram_tensor("arr", [1], F32, kind="ExternalInput").ap()
    kvT = nc.dram_tensor("kvT", [DM, 128], F32, kind="ExternalInput").ap()
    owT = nc.dram_tensor("owT", [64, DM], F32, kind="ExternalInput").ap()
    tsg = nc.dram_tensor("tsg", [DM], F32, kind="ExternalInput").ap()
    lnw = nc.dram_tensor("lnw", [DM], F32, kind="ExternalInput").ap()
    lnb = nc.dram_tensor("lnb", [DM], F32, kind="ExternalInput").ap()
    td = nc.dram_tensor("td", [64], F32, kind="ExternalInput").ap()
    tf = nc.dram_tensor("tf", [64], F32, kind="ExternalInput").ap()
    ucb = nc.dram_tensor("ucb", [256], F32, kind="Internal").ap()

    es = ExitStack()
    sb = lambda name, shape, dt: es.enter_context(nc.sbuf_tensor(name, shape, dt))

    xn = sb("xn", [128, NT * DC * T], BF16)
    xnF = xn[:].bitcast(F32)
    XL = [sb("XL0", [128, DC * T], F32), sb("XL1", [128, DC * T], F32)]
    Y = [sb("Y0", [64, S], BF16), sb("Y1", [64, S], BF16)]
    W1s = sb("W1s", [128, 64 * 128], BF16)
    W2s = sb("W2s", [128, 32 * 128], BF16)
    W3s = sb("W3s", [128, 8 * 128], BF16)
    W4s = sb("W4s", [128, 2], BF16)
    KVs = sb("KVs", [128, 8 * 128], BF16)
    OWs = sb("OWs", [64, 8 * 128], BF16)
    G3s = sb("G3s", [128, 2], BF16)
    H1 = sb("H1", [128, DC * T], BF16)
    KV0 = H1
    H2 = sb("H2", [128, 4 * T], BF16)
    H3 = sb("H3", [128, 2 * T], BF16)
    TTb = sb("TTb", [128, DC * T], BF16)
    SQ = sb("SQ", [128, DC * T], BF16)
    XB = sb("XB", [128, DC * T], BF16)
    SSB = sb("SSB", [128, DC * T], BF16)
    SSBf = SSB[:].bitcast(F32)
    stv = sb("stv", [1, 3 * T], F32)
    cplv = sb("cplv", [1, T], F32)
    CBs = sb("CBs", [128, 2 * T], BF16)
    thv = sb("thv", [1, T], F32)
    ESB = sb("ESB", [128, T], F32)
    E2 = ESB
    TMP = sb("TMP", [128, T], F32)
    P1B = sb("P1B", [128, T], F32)
    b1v = sb("b1v", [128, 8], F32)
    b2v = sb("b2v", [128, 4], F32)
    b3v = sb("b3v", [128, 2], F32)
    gb1v = sb("gb1v", [128, 4], F32)
    gb2v = sb("gb2v", [128, 2], F32)
    gvec = sb("gvec", [128, 8], F32)
    omgv = sb("omgv", [128, 8], F32)
    gtmp = sb("gtmp", [128, 8], F32)
    lnwv = sb("lnwv", [128, 8], F32)
    lnbv = sb("lnbv", [128, 8], F32)
    lnbB = sb("lnbB", [128, 8], BF16)
    uvec = sb("uvec", [128, 1], F32)
    urow = sb("urow", [1, 128], F32)
    cvec = sb("cvec", [128, 1], F32)
    tdv = sb("tdv", [128, 1], F32)
    evec = sb("evec", [128, 1], F32)
    negef = sb("negef", [128, 1], F32)
    biask = sb("biask", [128, 1], F32)
    GACC = sb("GACC", [128, 8], F32)
    GBF = sb("GBF", [128, 8], BF16)
    HG1 = sb("HG1", [128, 4], BF16)
    HG2 = sb("HG2", [128, 2], BF16)
    ACCS = sb("ACCS", [128, 8], F32)
    onesb = sb("onesb", [128, 1], BF16)
    onesf = sb("onesf", [1, 128], F32)
    misc = sb("misc", [1, 16], F32)

    PSA = es.enter_context(nc.psum_tensor("PSA", [128, 4 * T], F32))
    PSB = es.enter_context(nc.psum_tensor("PSB", [128, 4 * T], F32))

    def bank(i):
        return PSA[:, (i % 4) * T:(i % 4 + 1) * T] if i < 4 else \
               PSB[:, (i - 4) * T:(i - 4 + 1) * T]

    def bankrow(i, r):
        return bank(i)[r:r + 1, :]

    sch = Sched()

    def stslot(p, k):
        if k == 2:
            return stv[0:1, 2 * T:3 * T]
        if p == 0:
            return stv[0:1, k * T:(k + 1) * T]
        return cplv[0:1, 0:T] if k == 0 else thv[0:1, 0:T]

    def xntile(j, c0=0, c1=DC):
        return xn[:, j * DC * T + c0 * T: j * DC * T + c1 * T]

    def xnchunk(j, c):
        return xn[:, (j * DC + c) * T:(j * DC + c + 1) * T]

    def dma_w_to_stage(eng, w, rows, cols, dst_off):
        src = w.rearrange("(ci p) ck -> p ci ck", p=128)
        dst = xnF[:, dst_off:dst_off + (rows // 128) * cols].rearrange(
            "p (ci ck) -> p ci ck", ci=rows // 128)
        return [eng.dma_start(out=dst, in_=src, allow_slow_non_contiguous=True)]

    def em_pvec(eng):
        insts = []
        A = dict(allow_slow_non_contiguous=True)
        insts.append(eng.dma_start(out=lnwv[:], in_=lnw.rearrange("(c p) -> p c", p=128), **A))
        insts.append(eng.dma_start(out=lnbv[:], in_=lnb.rearrange("(c p) -> p c", p=128), **A))
        insts.append(eng.dma_start(out=b1v[:], in_=b1.rearrange("(c p) -> p c", p=128), **A))
        insts.append(eng.dma_start(out=b2v[:], in_=b2.rearrange("(c p) -> p c", p=128), **A))
        insts.append(eng.dma_start(out=b3v[:], in_=b3.rearrange("(c p) -> p c", p=128), **A))
        insts.append(eng.dma_start(out=gb1v[:], in_=gb1.rearrange("(c p) -> p c", p=128), **A))
        insts.append(eng.dma_start(out=gb2v[:], in_=gb2.rearrange("(c p) -> p c", p=128), **A))
        insts.append(eng.dma_start(out=gvec[:], in_=tsg.rearrange("(c p) -> p c", p=128), **A))
        insts.append(eng.dma_start(out=tdv[0:64, 0:1], in_=td.rearrange("(p o) -> p o", o=1), **A))
        insts.append(eng.dma_start(out=tdv[64:128, 0:1], in_=tf.rearrange("(p o) -> p o", o=1), **A))
        insts.append(eng.dma_start(out=misc[0:1, 0:1], in_=b4.rearrange("(o q) -> o q", q=1), **A))
        insts.append(eng.dma_start(out=misc[0:1, 1:2], in_=gb3.rearrange("(o q) -> o q", q=1), **A))
        insts.append(eng.dma_start(out=misc[0:1, 2:3], in_=arr.rearrange("(o q) -> o q", q=1), **A))
        return insts

    sch.add("sync", em_pvec, inc="LDV", n=13, amt=16, key="pvec",
            writes=("lnwv", "lnbv", "b1v", "b2v", "b3v", "gb1v", "gb2v",
                    "gvec", "tdv", "miscin"))

    for ej in (0, 1):
        def em_eax(eng, ej=ej):
            srcx = x_fm[0:1, :, ej * T:(ej + 1) * T].rearrange(
                "one (dc p) t -> p (one dc) t", p=128)
            return [eng.dma_start(
                out=XL[ej][:, 4 * q * T // 2:4 * (q + 1) * T // 2].rearrange(
                    "p (dc t) -> p dc t", dc=2),
                in_=srcx[:, 2 * q:2 * q + 2, :]) for q in (0, 1)]
        sch.add("sync", em_eax, inc=f"LD{ej}", n=2, amt=16, key=f"b0_a_x{ej}a",
                writes=(f"XL{ej}a",))

        def em_eax2(eng, ej=ej):
            srcx = x_fm[0:1, :, ej * T:(ej + 1) * T].rearrange(
                "one (dc p) t -> p (one dc) t", p=128)
            return [eng.dma_start(
                out=XL[ej][:, 4 * q * T // 2:4 * (q + 1) * T // 2].rearrange(
                    "p (dc t) -> p dc t", dc=2),
                in_=srcx[:, 2 * q:2 * q + 2, :]) for q in (2, 3)]
        sch.add("gpsimd", em_eax2, inc=f"LG{ej}", n=2, amt=16, key=f"b0_a_x{ej}",
                writes=(f"XL{ej}b",))

    sch.add("sync", lambda e: dma_w_to_stage(e, w1, DM, DM, 0),
            inc="LDS", n=1, amt=16, key="pw1", writes=("stage",))
    sch.add("sync", lambda e: dma_w_to_stage(e, w2, DM, 512, 8192),
            inc="LDS", n=1, amt=16, key="pw2", writes=("stage2",))
    sch.add("vector", lambda e: e.tensor_copy(W1s[:], xnF[:, 0:8192]),
            inc="VE", key="cw1", reads=("stage",), writes=("W1s",))
    sch.add("vector", lambda e: e.tensor_copy(W2s[:], xnF[:, 8192:12288]),
            inc="VE", key="cw2", reads=("stage2",), writes=("W2s",))
    sch.add("sync", lambda e: dma_w_to_stage(e, w3, 512, 256, 0),
            inc="LDS", n=1, amt=16, key="pw3", writes=("stage",))
    sch.add("sync", lambda e: dma_w_to_stage(e, w4, 256, 1, 8192),
            inc="LDS", n=1, amt=16, key="pw4", writes=("stage2",))
    sch.add("vector", lambda e: e.tensor_copy(W3s[:], xnF[:, 0:1024]),
            inc="VE", key="cw3", reads=("stage",), writes=("W3s",))
    sch.add("vector", lambda e: e.tensor_copy(W4s[:], xnF[:, 8192:8194]),
            inc="VE", key="cw4", reads=("stage2",), writes=("W4s",))
    sch.add("sync", lambda e: dma_w_to_stage(e, kvT, DM, 128, 0),
            inc="LDS", n=1, amt=16, key="pkv", writes=("stage",))
    sch.add("vector", lambda e: e.tensor_copy(KV0[:, 0:1024], xnF[:, 0:1024]),
            inc="VE", key="ckv0", reads=("stage",), writes=("H1",))

    def em_kvscale(eng):
        last = None
        for ci in range(8):
            last = eng.tensor_scalar_mul(
                KVs[:, ci * 128:(ci + 1) * 128],
                xnF[:, ci * 128:(ci + 1) * 128], lnwv[:, ci:ci + 1])
        return last
    sch.add("vector", em_kvscale, inc="VE", key="ckv",
            reads=("stage", "lnwv"), writes=("KVs",))

    sch.add("sync", lambda e: [e.dma_start(out=xnF[0:64, 0:1024], in_=owT)],
            inc="LDS", n=1, amt=16, key="pow", writes=("stage",))
    sch.add("vector", lambda e: e.tensor_copy(OWs[:], xnF[0:64, 0:1024]),
            inc="VE", key="cow", reads=("stage",), writes=("OWs",))
    sch.add("sync", lambda e: dma_w_to_stage(e, gw3, 256, 1, 0),
            inc="LDS", n=1, amt=16, key="pg3", writes=("stage",))
    sch.add("vector", lambda e: e.tensor_copy(G3s[:], xnF[:, 0:2]),
            inc="VE", key="cg3", reads=("stage",), writes=("G3s",))

    sch.add("vector", lambda e: e.tensor_copy(lnbB[:], lnbv[:]),
            inc="VE", key="clnb", reads=("lnbv",), writes=("lnbB",))

    def em_memsets(eng):
        eng.memset(onesb[:], 1.0)
        eng.memset(onesf[:], 1.0)
        return eng.memset(misc[0:1, 8:9], EPS)
    sch.add("vector", em_memsets, inc="VE", key="pmemset",
            writes=("onesb", "onesf", "epsc"))

    def em_umm(eng):
        insts = []
        for ci in range(8):
            insts.append(eng.matmul(bankrow(4, 0)[:, 0:128], onesb[:],
                                    KVs[:, ci * 128:(ci + 1) * 128],
                                    start=(ci == 0), stop=(ci == 7)))
        return insts[-1]
    sch.add("tensor", em_umm, inc="TE", key="t_u",
            reads=("KVs", "onesb"), writes=("pb4",))

    def em_cmm(eng):
        insts = []
        for ci in range(8):
            insts.append(eng.matmul(bankrow(5, 0)[:, 0:128], lnbB[:, ci:ci + 1],
                                    KV0[:, ci * 128:(ci + 1) * 128],
                                    start=(ci == 0), stop=(ci == 7)))
        return insts[-1]
    sch.add("tensor", em_cmm, inc="TE", key="t_c",
            reads=("H1", "lnbB"), writes=("pb5",))

    def em_evac_uc(eng):
        eng.tensor_copy(stv[0:1, 0:128], bankrow(4, 0)[:, 0:128])
        return eng.tensor_copy(stv[0:1, 128:256], bankrow(5, 0)[:, 0:128])
    sch.add("vector", em_evac_uc, inc="VE", key="e_uc",
            reads=("pb4", "pb5"), writes=("stv0",))

    sch.add("gpsimd", lambda e: [e.dma_start(out=ucb.rearrange("(o n) -> o n", o=1),
                                             in_=stv[0:1, 0:256])],
            inc="GPU", n=1, amt=16, key="m_uc1", reads=("stv0",))

    def em_dma_uc(eng):
        i1 = eng.dma_start(out=uvec[:], in_=ucb[0:128].rearrange("(p o) -> p o", o=1),
                           allow_slow_non_contiguous=True)
        i2 = eng.dma_start(out=cvec[:], in_=ucb[128:256].rearrange("(p o) -> p o", o=1),
                           allow_slow_non_contiguous=True)
        return [i1, i2]
    sch.add("gpsimd", em_dma_uc, waits=("m_uc1",), inc="GPU2", n=2, amt=16, key="m_uc",
            writes=("uvec", "cvec"))
    sch.add("gpsimd", lambda e: e.dma_start(out=urow[:],
                                            in_=ucb[0:128].rearrange("(o n) -> o n", o=1)),
            waits=("m_uc1",), inc="GPU3", amt=16, key="m_ur", writes=("urow",))

    sch.add("scalar", lambda e: e.activation(evec[:], tdv[:], Act.Exp),
            inc="SE", key="s_ev", reads=("tdv",), writes=("evec",))
    sch.add("vector", lambda e: e.tensor_scalar_mul(negef[64:128, :], evec[64:128, :], -1.0),
            inc="VE", key="v_negef", reads=("evec",), writes=("negef",))
    sch.add("vector", lambda e: e.tensor_mul(biask[64:128, :], negef[64:128, :], cvec[64:128, :]),
            inc="VE", key="v_biask", reads=("negef", "cvec"), writes=("biask",))

    sch.add("scalar", lambda e: e.activation(gtmp[:], gvec[:], Act.Exp, scale=-1.0),
            inc="SE", key="s_gex", reads=("gvec",), writes=("gtmp",))
    sch.add("vector", lambda e: e.tensor_scalar_add(gtmp[:], gtmp[:], 1.0),
            inc="VE", key="v_g1", reads=("gtmp",), writes=("gtmp",))
    sch.add("vector", lambda e: e.reciprocal(gvec[:], gtmp[:]),
            inc="VE", key="v_g2", reads=("gtmp",), writes=("gvec",))
    sch.add("vector", lambda e: e.tensor_scalar(omgv[:], gvec[:], -1.0, 1.0, Alu.mult, Alu.add),
            inc="VE", key="v_gates", reads=("gvec",), writes=("omgv",))

    def em_misc(eng):
        eng.tensor_scalar_mul(misc[0:1, 0:1], misc[0:1, 0:1], 0.5)
        return eng.tensor_scalar_mul(misc[0:1, 1:2], misc[0:1, 1:2], 0.5)
    sch.add("vector", em_misc, inc="VE", key="v_misc",
            reads=("miscin",), writes=("misc05",))

    def warmers(tag, n=2, wait_key=None):
        def em_w(eng):
            insts = []
            for i in range(n):
                insts.append(eng.matmul(bankrow(0, 0), onesb[:], SQ[:, 0:512],
                                        start=True, stop=True))
            return insts[-1]
        sch.add("tensor", em_w, waits=(wait_key,) if wait_key else (),
                inc="TE", key=tag)


    def phaseA(b):
        P = f"b{b}_"
        sch.add("vector", lambda e: e.memset(GACC[:], 0.0),
                inc="VE", key=P + "g0", writes=("GACC",))
        for j in range(NT):
            xl = XL[j % 2]
            rXL = f"XL{j % 2}"
            p = j % 2
            sA = 4 + p
            bMu = 6 + p

            def em_ax(eng, b=b, j=j, xl=xl):
                src = x_fm[b:b + 1, :, j * T:(j + 1) * T].rearrange(
                    "one (dc p) t -> p (one dc) t", p=128)
                return [eng.dma_start(
                    out=xl[:, 4 * q * T // 2:4 * (q + 1) * T // 2].rearrange(
                        "p (dc t) -> p dc t", dc=2),
                    in_=src[:, 2 * q:2 * q + 2, :]) for q in (0, 1)]
            if not (b == 0 and j <= 1):
                sch.add("sync", em_ax, inc=f"LD{p}", n=2, amt=16, key=P + f"a_x{j}a",
                        writes=(rXL + "a",))

            def em_ax2(eng, b=b, j=j, xl=xl):
                src = x_fm[b:b + 1, :, j * T:(j + 1) * T].rearrange(
                    "one (dc p) t -> p (one dc) t", p=128)
                return [eng.dma_start(
                    out=xl[:, 4 * q * T // 2:4 * (q + 1) * T // 2].rearrange(
                        "p (dc t) -> p dc t", dc=2),
                    in_=src[:, 2 * q:2 * q + 2, :]) for q in (2, 3)]
            if not (b == 0 and j <= 1):
                sch.add("gpsimd", em_ax2, inc=f"LG{p}", n=2, amt=16, key=P + f"a_x{j}",
                        writes=(rXL + "b",))

            xbb = XB if j % 2 == 0 else TTb
            sqb = SQ if j % 2 == 0 else H1
            rXB = "XB" if j % 2 == 0 else "TTb"
            rSQ = "SQ" if j % 2 == 0 else "H1"
            sch.add("scalar", lambda e, xl=xl, sqb=sqb: e.activation(sqb[:], xl[:], Act.Square),
                    inc="SE", key=P + f"a_sq{j}", reads=(rXL + "a", rXL + "b"), writes=(rSQ,))
            sch.add("scalar", lambda e, xl=xl, xbb=xbb: e.activation(xbb[:], xl[:], Act.Copy),
                    inc="SE", key=P + f"a_xb{j}", reads=(rXL + "a", rXL + "b"), writes=(rXB,))

            def em_amm(eng, sA=sA, xbb=xbb, sqb=sqb):
                insts = []
                for c in range(DC):
                    insts.append(eng.matmul(bankrow(sA, 0), onesb[:],
                                            xbb[:, c * T:(c + 1) * T],
                                            start=(c == 0), stop=(c == DC - 1)))
                for c in range(DC):
                    insts.append(eng.matmul(bankrow(sA, 32), onesb[:],
                                            sqb[:, c * T:(c + 1) * T],
                                            start=(c == 0), stop=(c == DC - 1)))
                return insts[-1]
            sch.add("tensor", em_amm, inc="TE", key=P + f"a_mm{j}",
                    reads=(rXB, rSQ, "onesb"), writes=(f"pb{sA}",))

            def em_afv(eng, sA=sA, p=p):
                eng.tensor_scalar_mul(stslot(p, 0), bankrow(sA, 0), 1.0 / DM)
                eng.tensor_scalar_mul(stslot(p, 1), bankrow(sA, 32), 1.0 / DM)
                eng.tensor_mul(stslot(p, 2), stslot(p, 0), stslot(p, 0))
                return eng.tensor_sub(stslot(p, 1), stslot(p, 1), stslot(p, 2))
            sch.add("vector", em_afv, inc="VE", key=P + f"a_fv{j}",
                    reads=(f"pb{sA}",), writes=("stv0",))

            def em_afs(eng, p=p):
                eng.activation(stslot(p, 2), stslot(p, 1), Act.Ln, bias=misc[0:1, 8:9])
                return eng.activation(stslot(p, 1), stslot(p, 2), Act.Exp, scale=-0.5)
            sch.add("scalar", em_afs, inc="SE", key=P + f"a_fs{j}",
                    reads=("stv0", "epsc"), writes=("stv0",))

            sch.add("tensor", lambda e, bMu=bMu, p=p: e.matmul(
                        bank(bMu), onesf[:], stslot(p, 0), start=True, stop=True),
                    inc="TE", key=P + f"a_bmu{j}",
                    reads=("stv0", "onesf"), writes=(f"pb{bMu}",))

            def em_ax1(eng, j=j, xl=xl, bMu=bMu):
                mu_b = bank(bMu).unsqueeze(1).broadcast_to([128, DC, T])
                xlv = xl[:].rearrange("p (dc t) -> p dc t", dc=DC)
                dst = xntile(j).rearrange("p (dc t) -> p dc t", dc=DC)
                return eng.scalar_tensor_tensor(dst, mu_b, -1.0, xlv, Alu.mult, Alu.add)
            sch.add("vector", em_ax1, inc="VE", key=P + f"a_x1{j}",
                    reads=(f"pb{bMu}", rXL + "a", rXL + "b"), writes=(f"xn{j}",))

            sch.add("tensor", lambda e, sA=sA, p=p: e.matmul(
                        bank(sA), onesf[:], stslot(p, 1), start=True, stop=True),
                    inc="TE", key=P + f"a_br{j}",
                    reads=("stv0", "onesf"), writes=(f"pb{sA}",))

            def em_axn(eng, j=j, sA=sA):
                last = None
                for c in range(DC):
                    last = eng.scalar_tensor_tensor(xnchunk(j, c), xnchunk(j, c), 1.0,
                                                    bank(sA), Alu.mult, Alu.mult,
                                                    accum_out=ACCS[:, c:c + 1])
                return last
            sch.add("vector", em_axn, inc="VE", key=P + f"a_xn{j}",
                    reads=(f"pb{sA}",), writes=(f"xn{j}", "ACCS"))
            sch.add("vector", lambda e: e.tensor_add(GACC[:], GACC[:], ACCS[:]),
                    inc="VE", key=P + f"a_ga{j}", reads=("ACCS",), writes=("GACC",))
            warmers(P + f"a_w{j}a", 3, P + f"a_sq{j}")
            warmers(P + f"a_w{j}b", 3, P + f"a_sq{j}")
            yield

    def phaseB(b):
        P = f"b{b}_"

        def tilevars(j):
            p = j % 2
            return dict(
                j=j, jj=(j + SHIFT_TILES) % NT, p=p, s2=j % 4, bA=4 + j % 4,
                xsb=XB if p == 0 else TTb,
                sqb=SQ if p == 0 else H1,
                rXS="XB" if p == 0 else "TTb",
                rSQ="SQ" if p == 0 else "H1",
                tmpb=TMP[:] if p == 0 else SSBf[:, 0:T],
                p1b=P1B[:] if p == 0 else SSBf[:, T:2 * T],
                esb=ESB[:] if p == 0 else SSBf[:, 2 * T:3 * T],
                rTMP=("TMP",) if p == 0 else ("TMP1", "SSB"),
                rP1=("P1B",) if p == 0 else ("P1B1", "SSB"),
                rES=("ESB",) if p == 0 else ("ESB1", "SSB"),
            )

        def stage1(v):
            j, jj, xsb, sqb = v["j"], v["jj"], v["xsb"], v["sqb"]
            bA, s2, rXS, rSQ = v["bA"], v["s2"], v["rXS"], v["rSQ"]

            def em_bxs(eng, j=j, jj=jj, xsb=xsb):
                eng.tensor_sub(xsb[:], xntile(jj), xntile(j))
                for c in range(DC):
                    eng.tensor_scalar_mul(xsb[:, c * T:(c + 1) * T],
                                          xsb[:, c * T:(c + 1) * T],
                                          gvec[:, c:c + 1])
                return eng.tensor_add(xsb[:], xsb[:], xntile(j))
            sch.add("vector", em_bxs, inc="VE", key=P + f"b_xs{j}",
                    reads=(f"xn{j}", f"xn{jj}", "gvec"), writes=(rXS,))

            sch.add("scalar", lambda e, xsb=xsb, sqb=sqb: e.activation(
                        sqb[:], xsb[:], Act.Square),
                    inc="SE", key=P + f"b_sq{j}", reads=(rXS,), writes=(rSQ,))

            def em_bmm(eng, bA=bA, s2=s2, xsb=xsb, sqb=sqb):
                insts = []
                for c in range(DC):
                    insts.append(eng.matmul(bank(bA), KVs[:, c * 128:(c + 1) * 128],
                                            xsb[:, c * T:(c + 1) * T],
                                            start=(c == 0), stop=False))
                for c in range(DC):
                    insts.append(eng.matmul(bankrow(s2, 0), onesb[:],
                                            xsb[:, c * T:(c + 1) * T],
                                            start=(c == 0), stop=(c == DC - 1)))
                for c in range(DC):
                    insts.append(eng.matmul(bankrow(s2, 32), onesb[:],
                                            sqb[:, c * T:(c + 1) * T],
                                            start=(c == 0), stop=(c == DC - 1)))
                return insts[-1]
            sch.add("tensor", em_bmm, inc="TE", key=P + f"b_mm{j}",
                    reads=(rXS, rSQ, "KVs", "onesb"), writes=(f"pb{bA}", f"pb{s2}"))

        def stage2a(v):
            j, p, s2, bA = v["j"], v["p"], v["s2"], v["bA"]

            def em_bfv(eng, s2=s2, p=p):
                eng.tensor_scalar_mul(stslot(p, 0), bankrow(s2, 0), -1.0 / DM)
                eng.tensor_scalar_mul(stslot(p, 1), bankrow(s2, 32), 1.0 / DM)
                eng.tensor_mul(stslot(p, 2), stslot(p, 0), stslot(p, 0))
                return eng.tensor_sub(stslot(p, 1), stslot(p, 1), stslot(p, 2))
            sch.add("vector", em_bfv, inc="VE", key=P + f"b_fv{j}",
                    reads=(f"pb{s2}",), writes=(f"stv{p}",))

            def em_bfs(eng, p=p):
                eng.activation(stslot(p, 2), stslot(p, 1), Act.Ln, bias=misc[0:1, 8:9])
                return eng.activation(stslot(p, 1), stslot(p, 2), Act.Exp, scale=-0.5)
            sch.add("scalar", em_bfs, inc="SE", key=P + f"b_fs{j}",
                    reads=(f"stv{p}", "epsc"), writes=(f"stv{p}",))

            def em_bbc(eng, bA=bA, s2=s2, p=p):
                i1 = eng.matmul(bank(bA), urow[:], stslot(p, 0), start=False, stop=True)
                i2 = eng.matmul(bank(s2), onesf[:], stslot(p, 1), start=True, stop=True)
                return [i1, i2]
            sch.add("tensor", em_bbc, inc="TE", key=P + f"b_bc{j}", n=2,
                    reads=(f"stv{p}", "onesf", "urow"), writes=(f"pb{bA}", f"pb{s2}"))

        def stage2b(v):
            j, s2, p1b, rP1 = v["j"], v["s2"], v["p1b"], v["rP1"]
            sch.add("scalar", lambda e, s2=s2, p1b=p1b: e.activation(
                        p1b, bank(s2), Act.Copy),
                    inc="SE", key=P + f"b_rs{j}", reads=(f"pb{s2}",), writes=rP1)

        def stage3a(v):
            j, bA, tmpb, p1b, esb = v["j"], v["bA"], v["tmpb"], v["p1b"], v["esb"]
            rTMP, rP1, rES = v["rTMP"], v["rP1"], v["rES"]

            sch.add("vector", lambda e, bA=bA, tmpb=tmpb, p1b=p1b: e.tensor_mul(
                        tmpb, bank(bA), p1b),
                    inc="VE", key=P + f"b_tmp{j}",
                    reads=(f"pb{bA}",) + rP1, writes=rTMP)

            sch.add("scalar", lambda e, tmpb=tmpb, esb=esb: e.activation(
                        esb[64:128, :], tmpb[64:128, :], Act.Exp,
                        bias=biask[64:128, 0:1], scale=negef[64:128, 0:1]),
                    inc="SE", key=P + f"b_exp{j}",
                    reads=rTMP + ("biask", "negef"), writes=rES)

            sch.add("scalar", lambda e, esb=esb: e.dma_start(out=esb[0:64, :],
                                                             in_=esb[64:128, :]),
                    inc="GPE", amt=16, key=P + f"b_e2{j}",
                    reads=rES, writes=rES)

        def stage3b(v):
            j, tmpb, esb = v["j"], v["tmpb"], v["esb"]
            rTMP, rES = v["rTMP"], v["rES"]

            def em_bwkv(eng, j=j, tmpb=tmpb, esb=esb):
                return eng.scalar_tensor_tensor(
                    XL[0][0:64, j * T:(j + 1) * T], tmpb[0:64, :], cvec[0:64, 0:1],
                    esb[0:64, :], Alu.add, Alu.mult)
            sch.add("vector", em_bwkv, inc="VE", key=P + f"b_wkv{j}",
                    reads=rTMP + rES + ("cvec",), writes=("XL0a", "XL0b"))

        vs = [tilevars(j) for j in range(NT)]
        for r in range(NT + 4):
            if r < NT:
                stage1(vs[r])
            if 0 <= r - 2 < NT:
                stage2a(vs[r - 2])
            if 0 <= r - 3 < NT:
                stage2b(vs[r - 3])
                stage3a(vs[r - 3])
            if 0 <= r - 4 < NT:
                stage3b(vs[r - 4])
            yield

    def phaseB_scan(b):
        P = f"b{b}_"

        def em_scan(eng, b=b):
            d0 = evec[0:64, 0:1].broadcast_to([64, S])
            return eng.tensor_tensor_scan(Y[b][0:64, :], d0, XL[0][0:64, :], 0.0,
                                          Alu.mult, Alu.add)
        sch.add("vector", em_scan, inc="VE", key=P + "scan",
                reads=("XL0a", "XL0b", "evec"), writes=(f"Y{b}",))

    def phase_gf(b):
        P = f"b{b}_"
        a_sl = slice(4, 5) if b == 0 else slice(10, 11)
        b_sl = slice(3, 4) if b == 0 else slice(9, 10)

        sch.add("vector", lambda e: e.tensor_scalar_mul(GBF[:], GACC[:], 1.0 / S),
                inc="VE", key=P + "g_gbf", reads=("GACC",), writes=("GBF",))

        def dma_gstage(eng, w, nr):
            src = w.rearrange("(ci p) ck -> p ci ck", p=128)
            dst = XL[1][:, 0:nr * w.shape[1]].rearrange("p (ci ck) -> p ci ck", ci=nr)
            return eng.dma_start(out=dst, in_=src)

        sch.add("sync", lambda e: dma_gstage(e, gw1, 8),
                inc="LD1", n=1, amt=16, key=P + "g_l1", writes=("XL1a", "XL1b"))
        sch.add("vector", lambda e: e.tensor_copy(H1[:, 0:4096], XL[1][:, 0:4096]),
                inc="VE", key=P + "g_c1", reads=("XL1a", "XL1b"), writes=("H1",))
        yield

        def em_gm1(eng):
            insts = []
            for co in range(4):
                for ci in range(8):
                    insts.append(eng.matmul(
                        bank(4)[:, co:co + 1],
                        H1[:, (ci * 4 + co) * 128:(ci * 4 + co + 1) * 128],
                        GBF[:, ci:ci + 1], start=(ci == 0), stop=(ci == 7)))
            return insts[-1]
        sch.add("tensor", em_gm1, inc="TE", key=P + "g_m1",
                reads=("H1", "GBF"), writes=("pb4",))

        def em_ge1(eng):
            last = None
            for co in range(4):
                last = eng.activation(HG1[:, co:co + 1], bank(4)[:, co:co + 1],
                                      Act.Gelu, bias=gb1v[:, co:co + 1])
            return last
        sch.add("scalar", em_ge1, inc="SE", key=P + "g_e1",
                reads=("pb4", "gb1v"), writes=("HG1",))

        sch.add("sync", lambda e: dma_gstage(e, gw2, 4),
                inc="LD1", n=1, amt=16, key=P + "g_l2", writes=("XL1a", "XL1b"))
        sch.add("vector", lambda e: e.tensor_copy(H2[:, 0:1024], XL[1][:, 0:1024]),
                inc="VE", key=P + "g_c2", reads=("XL1a", "XL1b"), writes=("H2",))
        yield

        def em_gm2(eng):
            insts = []
            for co in range(2):
                for ci in range(4):
                    insts.append(eng.matmul(
                        bank(5)[:, co:co + 1],
                        H2[:, (ci * 2 + co) * 128:(ci * 2 + co + 1) * 128],
                        HG1[:, ci:ci + 1], start=(ci == 0), stop=(ci == 3)))
            return insts[-1]
        sch.add("tensor", em_gm2, inc="TE", key=P + "g_m2",
                reads=("H2", "HG1"), writes=("pb5",))

        def em_ge2(eng):
            last = None
            for co in range(2):
                last = eng.activation(HG2[:, co:co + 1], bank(5)[:, co:co + 1],
                                      Act.Gelu, bias=gb2v[:, co:co + 1])
            return last
        sch.add("scalar", em_ge2, inc="SE", key=P + "g_e2",
                reads=("pb5", "gb2v"), writes=("HG2",))

        def em_gm3(eng):
            eng.matmul(bankrow(6, 0)[:, 0:1], G3s[:, 0:1], HG2[:, 0:1],
                       start=True, stop=False)
            return eng.matmul(bankrow(6, 0)[:, 0:1], G3s[:, 1:2], HG2[:, 1:2],
                              start=False, stop=True)
        sch.add("tensor", em_gm3, inc="TE", key=P + "g_m3",
                reads=("HG2", "G3s"), writes=("pb6",))

        sch.add("scalar", lambda e: e.activation(misc[0:1, 6:7], bankrow(6, 0)[:, 0:1],
                                                 Act.Tanh, bias=misc[0:1, 1:2], scale=0.5),
                inc="SE", key=P + "g_thg", reads=("pb6", "misc05"), writes=("miscg",))

        def em_gab1(eng, b_sl=b_sl):
            eng.tensor_scalar(misc[0:1, 5:6], misc[0:1, 6:7], 0.5, 0.5, Alu.mult, Alu.add)
            eng.tensor_scalar_mul(misc[0:1, b_sl], misc[0:1, 2:3], 0.3)
            return eng.tensor_scalar(misc[0:1, 7:8], misc[0:1, 2:3], -0.7, 0.1,
                                     Alu.mult, Alu.add)
        sch.add("vector", em_gab1, inc="VE", key=P + "g_ab1",
                reads=("miscg", "miscin"), writes=("miscg2",))
        sch.add("vector", lambda e, a_sl=a_sl: e.tensor_mul(
                    misc[0:1, a_sl], misc[0:1, 2:3], misc[0:1, 5:6]),
                inc="VE", key=P + "g_ab2", reads=("miscg2",), writes=("miscg3",))
        sch.add("vector", lambda e, a_sl=a_sl: e.tensor_scalar(
                    misc[0:1, a_sl], misc[0:1, a_sl], 1.4, misc[0:1, 7:8],
                    Alu.mult, Alu.add),
                inc="VE", key=P + "g_ab", reads=("miscg3",), writes=(f"miscab{b}",))
        yield

    def chain_pair(b, cyc, j0):
        P = f"b{b}_"
        a_sl = slice(4, 5) if b == 0 else slice(10, 11)
        b_sl = slice(3, 4) if b == 0 else slice(9, 10)

        H1b = [H1, XB]
        H2b = [H2, SQ[:].rearrange("p (a t) -> p a t", a=4)[:, 0:4, :].rearrange("p a t -> p (a t)") if False else H2, ]
        def h1buf(s):
            return H1 if s == 0 else XB
        def h2buf(s):
            return H2 if s == 0 else SQ[:, 0:4 * T]
        def h3buf(s):
            return H3 if s == 0 else SQ[:, 4 * T:6 * T]
        def h1res(s):
            return "H1" if s == 0 else "XB"
        def h2res(s):
            return "H2" if s == 0 else "SQ"
        def h3res(s):
            return "H3" if s == 0 else "SQ"

        def K(s):
            return P + f"c{cyc}_{j0 + s}_"

        pi = (j0 // 2) % 2
        if pi == 0:
            thbufs = [thv[:], cplv[:]]
            thres = ["thv", "cplv2"]
        else:
            thbufs = [stv[0:1, 0:T], stv[0:1, T:2 * T]]
            thres = ["stv0", "stv0"]

        def em_h1r(eng, j, bb, cos):
            insts = []
            for ii, co in enumerate(cos):
                for ci in range(DC):
                    insts.append(eng.matmul(
                        bank(bb + ii),
                        W1s[:, (ci * 8 + co) * 128:(ci * 8 + co + 1) * 128],
                        xnchunk(j, ci), start=(ci == 0), stop=(ci == DC - 1)))
            return insts[-1]

        def em_e1r(eng, s, bb, cos):
            last = None
            for ii, co in enumerate(cos):
                last = eng.activation(h1buf(s)[:, co * T:(co + 1) * T],
                                      bank(bb + ii), Act.Gelu,
                                      bias=b1v[:, co:co + 1])
            return last

        pbs = lambda bb: tuple(f"pb{bb + k}" for k in range(4))

        for rnd, cos in ((0, (0, 1, 2, 3)), (1, (4, 5, 6, 7))):
            for s in (0, 1):
                j = j0 + s
                bb = 4 * s
                sch.add("tensor", lambda e, j=j, bb=bb, cos=cos: em_h1r(e, j, bb, cos),
                        inc="TE", key=K(s) + f"h1{rnd}",
                        reads=(f"xn{j}", "W1s"), writes=pbs(bb))
                sch.add("scalar", lambda e, s=s, bb=bb, cos=cos: em_e1r(e, s, bb, cos),
                        inc="SE", key=K(s) + f"e1{rnd}",
                        reads=pbs(bb) + ("b1v",), writes=(h1res(s),))

        for s in (0, 1):
            bb = 4 * s

            def em_h2(eng, s=s, bb=bb):
                insts = []
                for co in range(4):
                    for ci in range(DC):
                        insts.append(eng.matmul(
                            bank(bb + co),
                            W2s[:, (ci * 4 + co) * 128:(ci * 4 + co + 1) * 128],
                            h1buf(s)[:, ci * T:(ci + 1) * T],
                            start=(ci == 0), stop=(ci == DC - 1)))
                return insts[-1]
            sch.add("tensor", em_h2, inc="TE", key=K(s) + "h2",
                    reads=(h1res(s), "W2s"), writes=pbs(bb))

            def em_e2c(eng, s=s, bb=bb):
                last = None
                for co in range(4):
                    last = eng.activation(h2buf(s)[:, co * T:(co + 1) * T],
                                          bank(bb + co), Act.Gelu,
                                          bias=b2v[:, co:co + 1])
                return last
            sch.add("scalar", em_e2c, inc="SE", key=K(s) + "e2",
                    reads=pbs(bb) + ("b2v",), writes=(h2res(s),))

        for s in (0, 1):
            bb = 4 * s

            def em_h3(eng, s=s, bb=bb):
                insts = []
                for co in range(2):
                    for ci in range(4):
                        insts.append(eng.matmul(
                            bank(bb + co),
                            W3s[:, (ci * 2 + co) * 128:(ci * 2 + co + 1) * 128],
                            h2buf(s)[:, ci * T:(ci + 1) * T],
                            start=(ci == 0), stop=(ci == 3)))
                return insts[-1]
            sch.add("tensor", em_h3, inc="TE", key=K(s) + "h3",
                    reads=(h2res(s), "W3s"), writes=(f"pb{bb}", f"pb{bb + 1}"))

            def em_e3(eng, s=s, bb=bb):
                last = None
                for co in range(2):
                    last = eng.activation(h3buf(s)[:, co * T:(co + 1) * T],
                                          bank(bb + co), Act.Gelu,
                                          bias=b3v[:, co:co + 1])
                return last
            sch.add("scalar", em_e3, inc="SE", key=K(s) + "e3",
                    reads=(f"pb{bb}", f"pb{bb + 1}", "b3v"), writes=(h3res(s),))

        for s in (0, 1):
            bb = 4 * s

            def em_h4(eng, s=s, bb=bb):
                eng.matmul(bankrow(bb + 2, 0), W4s[:, 0:1], h3buf(s)[:, 0:T],
                           start=True, stop=False)
                return eng.matmul(bankrow(bb + 2, 0), W4s[:, 1:2],
                                  h3buf(s)[:, T:2 * T], start=False, stop=True)
            sch.add("tensor", em_h4, inc="TE", key=K(s) + "h4",
                    reads=(h3res(s), "W4s"), writes=(f"pb{bb + 2}",))

            sch.add("scalar", lambda e, s=s, bb=bb: e.activation(
                        thbufs[s], bankrow(bb + 2, 0), Act.Tanh,
                        bias=misc[0:1, 0:1], scale=0.5),
                    inc="SE", key=K(s) + "th",
                    reads=(f"pb{bb + 2}", "misc05"),
                    writes=(thres[s], "stv1"))

        def tail():
            for s in (0, 1):
                bb = 4 * s
                thbuf = thbufs[s]

                sch.add("vector", lambda e, thbuf=thbuf: e.tensor_scalar(
                            thbuf, thbuf, misc[0:1, b_sl], misc[0:1, a_sl],
                            Alu.mult, Alu.add),
                        inc="VE", key=K(s) + "cpl",
                        reads=(thres[s], f"miscab{b}"), writes=(thres[s],))

                sch.add("tensor", lambda e, bb=bb, thbuf=thbuf: e.matmul(
                            bank(bb + 3), onesf[:], thbuf, start=True, stop=True),
                        inc="TE", key=K(s) + "cbc",
                        reads=(thres[s], "onesf"), writes=(f"pb{bb + 3}",))

                sch.add("scalar", lambda e, s=s, bb=bb: e.activation(
                            CBs[:, s * T:(s + 1) * T], bank(bb + 3), Act.Copy),
                        inc="SE", key=K(s) + "cbe",
                        reads=(f"pb{bb + 3}",), writes=(f"CB{s}",))

            for s in (0, 1):
                j = j0 + s

                def em_um(eng, j=j, s=s):
                    base = j * DC * T
                    cb = CBs[:, s * T:(s + 1) * T].unsqueeze(1).broadcast_to([128, 4, T])
                    eng.tensor_tensor(
                        TTb[:, 0:4 * T].rearrange("p (a t) -> p a t", a=4),
                        xn[:, base + 4 * T:base + 8 * T].rearrange("p (a t) -> p a t", a=4),
                        cb, Alu.mult)
                    return eng.tensor_tensor(
                        TTb[:, 4 * T:8 * T].rearrange("p (a t) -> p a t", a=4),
                        xn[:, base:base + 4 * T].rearrange("p (a t) -> p a t", a=4),
                        cb, Alu.mult)
                sch.add("vector", em_um, inc="VE", key=K(s) + "um",
                        reads=(f"CB{s}", f"xn{j}"), writes=("TTb",))

                def em_upd(eng, j=j):
                    base = j * DC * T
                    eng.tensor_add(xn[:, base:base + 2 * T],
                                   xn[:, base:base + 2 * T], TTb[:, 0:2 * T])
                    eng.tensor_sub(xn[:, base + 2 * T:base + 6 * T],
                                   xn[:, base + 2 * T:base + 6 * T], TTb[:, 2 * T:6 * T])
                    return eng.tensor_add(xn[:, base + 6 * T:base + 8 * T],
                                          xn[:, base + 6 * T:base + 8 * T],
                                          TTb[:, 6 * T:8 * T])
                sch.add("vector", em_upd, inc="VE", key=K(s) + "upd",
                        reads=("TTb",), writes=(f"xn{j}",))
        return tail

    def phaseD_load(b, j):
        P = f"b{b}_"
        K = P + f"d{j}_"
        xl = XL[j % 2]
        rXL = f"XL{j % 2}"

        def em_dx(eng, b=b, j=j, xl=xl):
            src = x_fm[b:b + 1, :, j * T:(j + 1) * T].rearrange(
                "one (dc p) t -> p (one dc) t", p=128)
            return [eng.dma_start(
                out=xl[:, 4 * q * T // 2:4 * (q + 1) * T // 2].rearrange(
                    "p (dc t) -> p dc t", dc=2),
                in_=src[:, 2 * q:2 * q + 2, :]) for q in (0, 1)]
        sch.add("sync", em_dx, inc=f"LD{j % 2}", n=2, amt=16, key=K + "xa",
                writes=(rXL + "a",))

        def em_dx2(eng, b=b, j=j, xl=xl):
            src = x_fm[b:b + 1, :, j * T:(j + 1) * T].rearrange(
                "one (dc p) t -> p (one dc) t", p=128)
            return [eng.dma_start(
                out=xl[:, 4 * q * T // 2:4 * (q + 1) * T // 2].rearrange(
                    "p (dc t) -> p dc t", dc=2),
                in_=src[:, 2 * q:2 * q + 2, :]) for q in (2, 3)]
        sch.add("gpsimd", em_dx2, inc=f"LG{j % 2}", n=2, amt=16, key=K + "x",
                writes=(rXL + "b",))

    def phaseD(b, j, split_tail=False):
        P = f"b{b}_"
        K = P + f"d{j}_"
        xl = XL[j % 2]
        rXL = f"XL{j % 2}"

        sch.add("scalar", lambda e, j=j: e.activation(xntile(j), xntile(j), Act.Silu),
                inc="SE", key=K + "g", reads=(f"xn{j}",), writes=(f"xn{j}",))

        for rnd in range(2):
            bb = 4 * rnd
            ps = PSA if rnd == 0 else PSB

            def em_dmm(eng, b=b, j=j, rnd=rnd, bb=bb):
                insts = []
                for c in range(4 * rnd, 4 * rnd + 4):
                    insts.append(eng.matmul(bank(bb + c % 4),
                                            OWs[:, c * 128:(c + 1) * 128],
                                            Y[b][0:64, j * T:(j + 1) * T],
                                            start=True, stop=True))
                return insts[-1]
            sch.add("tensor", em_dmm, inc="TE", key=K + f"mm{rnd}",
                    reads=(f"Y{b}", "OWs"),
                    writes=tuple(f"pb{bb + k}" for k in range(4)))

            def em_dmul(eng, j=j, rnd=rnd, ps=ps):
                base = j * DC * T + rnd * 4 * T
                eng.tensor_mul(SSB[:, rnd * 4 * T:rnd * 4 * T + 2 * T],
                               ps[:, 0:2 * T], xn[:, base:base + 2 * T])
                return eng.tensor_mul(SSB[:, rnd * 4 * T + 2 * T:rnd * 4 * T + 4 * T],
                                      ps[:, 2 * T:4 * T], xn[:, base + 2 * T:base + 4 * T])
            sch.add("vector", em_dmul, inc="VE", key=K + f"mul{rnd}",
                    waits=(K + "g",),
                    reads=tuple(f"pb{bb + k}" for k in range(4)) + (f"xn{j}",),
                    writes=("SSB",))

        def em_dst_half(eng, rnd, b=b, j=j, xl=xl):
            dst = y_fm[b:b + 1, :, j * T:(j + 1) * T].rearrange(
                "one (dc p) t -> p (one dc) t", p=128)
            lo = rnd * 4 * T
            return eng.dma_start(
                out=dst[:, 4 * rnd:4 * rnd + 4, :],
                in_=xl[:, lo:lo + 4 * T].rearrange("p (dc t) -> p dc t", dc=4))

        if split_tail:
            for rnd, half in ((0, "a"), (1, "b")):
                lo = rnd * 4 * T
                rH = rXL + half
                sch.add("vector", lambda e, xl=xl, lo=lo: e.tensor_add(
                            xl[:, lo:lo + 4 * T], SSB[:, lo:lo + 4 * T],
                            xl[:, lo:lo + 4 * T]),
                        inc="VE", key=K + "add" + half,
                        waits=(K + ("xa" if rnd == 0 else "x"),),
                        reads=("SSB", rH), writes=(rH,))
                sch.add("gpsimd", lambda e, rnd=rnd: em_dst_half(e, rnd),
                        inc=f"GPS{j % 2}", n=1, amt=16, key=K + "st" + half,
                        reads=(rH,))
        else:
            sch.add("vector", lambda e, xl=xl: e.tensor_add(xl[:], SSB[:], xl[:]),
                    inc="VE", key=K + "add", waits=(K + "x", K + "xa"),
                    reads=("SSB", rXL + "a", rXL + "b"),
                    writes=(rXL + "a", rXL + "b"))

            def em_dst(eng):
                return [em_dst_half(eng, 0), em_dst_half(eng, 1)]
            sch.add("gpsimd", em_dst, inc=f"GPS{j % 2}", n=2, amt=16, key=K + "st",
                    reads=(rXL + "a", rXL + "b"))

    for b in range(NB):
        for _ in phaseA(b):
            pass
        for _ in phaseB(b):
            pass
        for _ in phase_gf(b):
            pass
        phaseB_scan(b)
        pending_tail = None
        for cyc in range(3):
            if cyc == 1:
                phaseD_load(b, 0)
                phaseD_load(b, 1)
            for j0 in range(0, NT, 2):
                t = chain_pair(b, cyc, j0)
                if pending_tail is not None:
                    pending_tail()
                pending_tail = t
                if cyc == 2 and j0 >= 2:
                    phaseD(b, j0 - 2)
                    phaseD(b, j0 - 1)
                    phaseD_load(b, j0)
                    phaseD_load(b, j0 + 1)
        pending_tail()
        phaseD(b, NT - 2, split_tail=True)
        phaseD(b, NT - 1, split_tail=True)

    sch.add("gpsimd", lambda e: None,
            waits=(f"b1_d{NT-2}_sta", f"b1_d{NT-2}_stb",
                   f"b1_d{NT-1}_sta", f"b1_d{NT-1}_stb"))

    sch.finalize(nc, es)
    block = es.enter_context(nc.Block())
    block.sync(sch.emit_engine("sync"))
    block.gpsimd(sch.emit_engine("gpsimd"))
    block.scalar(sch.emit_engine("scalar"))
    block.vector(sch.emit_engine("vector"))
    block.tensor(sch.emit_engine("tensor"))
    es.close()
    return nc


def kernel(**inputs):
    x = np.asarray(inputs["L0_attn_x"])
    shared = dict(
        w1=np.ascontiguousarray(inputs["L0_ffn_cn_w1"], np.float32),
        w2=np.ascontiguousarray(inputs["L0_ffn_cn_w2"], np.float32),
        w3=np.ascontiguousarray(inputs["L0_ffn_cn_w3"], np.float32),
        w4=np.ascontiguousarray(inputs["L0_ffn_cn_w4"], np.float32),
        b1=np.ascontiguousarray(inputs["L0_ffn_cn_b1"], np.float32),
        b2=np.ascontiguousarray(inputs["L0_ffn_cn_b2"], np.float32),
        b3=np.ascontiguousarray(inputs["L0_ffn_cn_b3"], np.float32),
        b4=np.ascontiguousarray(inputs["L0_ffn_cn_b4"], np.float32),
        gw1=np.ascontiguousarray(inputs["L0_ffn_gc_w1"], np.float32),
        gw2=np.ascontiguousarray(inputs["L0_ffn_gc_w2"], np.float32),
        gw3=np.ascontiguousarray(inputs["L0_ffn_gc_w3"], np.float32),
        gb1=np.ascontiguousarray(inputs["L0_ffn_gc_b1"], np.float32),
        gb2=np.ascontiguousarray(inputs["L0_ffn_gc_b2"], np.float32),
        gb3=np.ascontiguousarray(inputs["L0_ffn_gc_b3"], np.float32),
        arr=np.ascontiguousarray(inputs["L0_ffn_adaptive_range"], np.float32),
        kvT=np.ascontiguousarray(
            np.concatenate([inputs["L0_attn_value_w"], inputs["L0_attn_key_w"]], 0).T,
            np.float32),
        owT=np.ascontiguousarray(np.asarray(inputs["L0_attn_output_w"]).T, np.float32),
        tsg=np.ascontiguousarray(inputs["L0_attn_time_shift_gates"], np.float32),
        lnw=np.ascontiguousarray(inputs["L0_attn_ln_w"], np.float32),
        lnb=np.ascontiguousarray(inputs["L0_attn_ln_b"], np.float32),
        td=np.ascontiguousarray(inputs["L0_attn_time_decay"], np.float32),
        tf=np.ascontiguousarray(inputs["L0_attn_time_first"], np.float32),
    )
    in_maps = []
    for i in range(8):
        xi = np.ascontiguousarray(np.swapaxes(x[2 * i:2 * i + 2], 1, 2), np.float32)
        m = dict(shared)
        m["x_fm"] = xi
        in_maps.append(m)

    nc = build_nc()
    res = run_bass_kernel_spmd(nc, in_maps, core_ids=list(range(8)))
    outs = [np.ascontiguousarray(np.swapaxes(r["y_fm"], 1, 2))
            for r in res.results]
    return np.concatenate(outs, 0).astype(np.float32)


# version 37 (final)
# speedup vs baseline: 1.0122x; 1.0122x over previous
import numpy as np
from contextlib import ExitStack
from collections import deque

import concourse.bass as bass
import concourse.mybir as mybir
from concourse.bass_utils import run_bass_kernel_spmd

F32 = mybir.dt.float32
BF16 = mybir.dt.bfloat16
Alu = mybir.AluOpType
Act = mybir.ActivationFunctionType

S = 4096
DM = 1024
NB = 2
T = 512
NT = S // T
DC = DM // 128
EPS = 1e-5
SHIFT_TILES = 4


class Sched:
    def __init__(self):
        self.steps = []
        self._writers = {}
        self._readers = {}

    def add(self, eng, emit, waits=(), inc=None, n=1, amt=1, key=None,
            reads=(), writes=()):
        w = list(waits)
        for r in writes:
            w += self._readers.get(r, [])
            if r in self._writers:
                w.append(self._writers[r])
        for r in reads:
            if r in self._writers:
                w.append(self._writers[r])
        self.steps.append(dict(eng=eng, emit=emit, waits=tuple(dict.fromkeys(w)),
                               inc=inc, n=n, amt=amt, key=key))
        if key is not None:
            for r in writes:
                self._writers[r] = key
                self._readers[r] = []
            for r in reads:
                self._readers.setdefault(r, []).append(key)

    def finalize(self, nc, es):
        sem_names = sorted({st["inc"] for st in self.steps if st["inc"]})
        self.handles = {s: es.enter_context(nc.semaphore(s)) for s in sem_names}
        cnt = {s: 0 for s in sem_names}
        marks = {}
        for st in self.steps:
            if st["inc"]:
                cnt[st["inc"]] += st["n"] * st["amt"]
                if st["key"]:
                    marks[st["key"]] = (st["inc"], cnt[st["inc"]])
        self.marks = marks

    def emit_engine(self, engname):
        marks, handles = self.marks, self.handles

        def body(eng):
            hw = {}
            for st in self.steps:
                if st["eng"] != engname:
                    continue
                for w in st["waits"]:
                    if w not in marks:
                        continue
                    sname, v = marks[w]
                    if hw.get(sname, 0) >= v:
                        continue
                    eng.wait_ge(handles[sname], v)
                    hw[sname] = v
                insts = st["emit"](eng)
                if st["inc"]:
                    if insts is None:
                        insts = []
                    if not isinstance(insts, (list, tuple)):
                        insts = [insts]
                    assert len(insts) == st["n"], (
                        f"step {st['key']}: expected {st['n']} incs, got {len(insts)}")
                    for i in insts:
                        i.then_inc(handles[st["inc"]], st["amt"])
        return body


def build_nc():
    nc = bass.Bass("TRN2", target_bir_lowering=False, detect_race_conditions=False)

    x_fm = nc.dram_tensor("x_fm", [NB, DM, S], F32, kind="ExternalInput").ap()
    y_fm = nc.dram_tensor("y_fm", [NB, DM, S], F32, kind="ExternalOutput").ap()
    w1 = nc.dram_tensor("w1", [DM, DM], F32, kind="ExternalInput").ap()
    w2 = nc.dram_tensor("w2", [DM, 512], F32, kind="ExternalInput").ap()
    w3 = nc.dram_tensor("w3", [512, 256], F32, kind="ExternalInput").ap()
    w4 = nc.dram_tensor("w4", [256, 1], F32, kind="ExternalInput").ap()
    b1 = nc.dram_tensor("b1", [DM], F32, kind="ExternalInput").ap()
    b2 = nc.dram_tensor("b2", [512], F32, kind="ExternalInput").ap()
    b3 = nc.dram_tensor("b3", [256], F32, kind="ExternalInput").ap()
    b4 = nc.dram_tensor("b4", [1], F32, kind="ExternalInput").ap()
    gw1 = nc.dram_tensor("gw1", [DM, 512], F32, kind="ExternalInput").ap()
    gw2 = nc.dram_tensor("gw2", [512, 256], F32, kind="ExternalInput").ap()
    gw3 = nc.dram_tensor("gw3", [256, 1], F32, kind="ExternalInput").ap()
    gb1 = nc.dram_tensor("gb1", [512], F32, kind="ExternalInput").ap()
    gb2 = nc.dram_tensor("gb2", [256], F32, kind="ExternalInput").ap()
    gb3 = nc.dram_tensor("gb3", [1], F32, kind="ExternalInput").ap()
    arr = nc.dram_tensor("arr", [1], F32, kind="ExternalInput").ap()
    kvT = nc.dram_tensor("kvT", [DM, 128], F32, kind="ExternalInput").ap()
    owT = nc.dram_tensor("owT", [64, DM], F32, kind="ExternalInput").ap()
    tsg = nc.dram_tensor("tsg", [DM], F32, kind="ExternalInput").ap()
    lnw = nc.dram_tensor("lnw", [DM], F32, kind="ExternalInput").ap()
    lnb = nc.dram_tensor("lnb", [DM], F32, kind="ExternalInput").ap()
    td = nc.dram_tensor("td", [64], F32, kind="ExternalInput").ap()
    tf = nc.dram_tensor("tf", [64], F32, kind="ExternalInput").ap()
    ucb = nc.dram_tensor("ucb", [256], F32, kind="Internal").ap()

    es = ExitStack()
    sb = lambda name, shape, dt: es.enter_context(nc.sbuf_tensor(name, shape, dt))

    xn = sb("xn", [128, NT * DC * T], BF16)
    xnF = xn[:].bitcast(F32)
    XL = [sb("XL0", [128, DC * T], F32), sb("XL1", [128, DC * T], F32)]
    Y = [sb("Y0", [64, S], BF16), sb("Y1", [64, S], BF16)]
    W1s = sb("W1s", [128, 64 * 128], BF16)
    W2s = sb("W2s", [128, 32 * 128], BF16)
    W3s = sb("W3s", [128, 8 * 128], BF16)
    W4s = sb("W4s", [128, 2], BF16)
    KVs = sb("KVs", [128, 8 * 128], BF16)
    OWs = sb("OWs", [64, 8 * 128], BF16)
    G3s = sb("G3s", [128, 2], BF16)
    H1 = sb("H1", [128, DC * T], BF16)
    KV0 = H1
    H2 = sb("H2", [128, 4 * T], BF16)
    H3 = sb("H3", [128, 2 * T], BF16)
    TTb = sb("TTb", [128, DC * T], BF16)
    SQ = sb("SQ", [128, DC * T], BF16)
    XB = sb("XB", [128, DC * T], BF16)
    SSB = sb("SSB", [128, DC * T], BF16)
    SSBf = SSB[:].bitcast(F32)
    stv = sb("stv", [1, 3 * T], F32)
    cplv = sb("cplv", [1, T], F32)
    CBs = sb("CBs", [128, 2 * T], BF16)
    thv = sb("thv", [1, T], F32)
    ESB = sb("ESB", [128, T], F32)
    E2 = ESB
    TMP = sb("TMP", [128, T], F32)
    P1B = sb("P1B", [128, T], F32)
    b1v = sb("b1v", [128, 8], F32)
    b2v = sb("b2v", [128, 4], F32)
    b3v = sb("b3v", [128, 2], F32)
    gb1v = sb("gb1v", [128, 4], F32)
    gb2v = sb("gb2v", [128, 2], F32)
    gvec = sb("gvec", [128, 8], F32)
    omgv = sb("omgv", [128, 8], F32)
    gtmp = sb("gtmp", [128, 8], F32)
    lnwv = sb("lnwv", [128, 8], F32)
    lnbv = sb("lnbv", [128, 8], F32)
    lnbB = sb("lnbB", [128, 8], BF16)
    uvec = sb("uvec", [128, 1], F32)
    urow = sb("urow", [1, 128], F32)
    cvec = sb("cvec", [128, 1], F32)
    tdv = sb("tdv", [128, 1], F32)
    evec = sb("evec", [128, 1], F32)
    negef = sb("negef", [128, 1], F32)
    biask = sb("biask", [128, 1], F32)
    GACC = sb("GACC", [128, 8], F32)
    GBF = sb("GBF", [128, 8], BF16)
    HG1 = sb("HG1", [128, 4], BF16)
    HG2 = sb("HG2", [128, 2], BF16)
    ACCS = sb("ACCS", [128, 8], F32)
    onesb = sb("onesb", [128, 1], BF16)
    onesf = sb("onesf", [1, 128], F32)
    misc = sb("misc", [1, 16], F32)

    PSA = es.enter_context(nc.psum_tensor("PSA", [128, 4 * T], F32))
    PSB = es.enter_context(nc.psum_tensor("PSB", [128, 4 * T], F32))

    def bank(i):
        return PSA[:, (i % 4) * T:(i % 4 + 1) * T] if i < 4 else \
               PSB[:, (i - 4) * T:(i - 4 + 1) * T]

    def bankrow(i, r):
        return bank(i)[r:r + 1, :]

    sch = Sched()

    def stslot(p, k):
        if k == 2:
            return stv[0:1, 2 * T:3 * T]
        if p == 0:
            return stv[0:1, k * T:(k + 1) * T]
        return cplv[0:1, 0:T] if k == 0 else thv[0:1, 0:T]

    def xntile(j, c0=0, c1=DC):
        return xn[:, j * DC * T + c0 * T: j * DC * T + c1 * T]

    def xnchunk(j, c):
        return xn[:, (j * DC + c) * T:(j * DC + c + 1) * T]

    def dma_w_to_stage(eng, w, rows, cols, dst_off):
        src = w.rearrange("(ci p) ck -> p ci ck", p=128)
        dst = xnF[:, dst_off:dst_off + (rows // 128) * cols].rearrange(
            "p (ci ck) -> p ci ck", ci=rows // 128)
        return [eng.dma_start(out=dst, in_=src, allow_slow_non_contiguous=True)]

    def em_pvec(eng):
        insts = []
        A = dict(allow_slow_non_contiguous=True)
        insts.append(eng.dma_start(out=lnwv[:], in_=lnw.rearrange("(c p) -> p c", p=128), **A))
        insts.append(eng.dma_start(out=lnbv[:], in_=lnb.rearrange("(c p) -> p c", p=128), **A))
        insts.append(eng.dma_start(out=b1v[:], in_=b1.rearrange("(c p) -> p c", p=128), **A))
        insts.append(eng.dma_start(out=b2v[:], in_=b2.rearrange("(c p) -> p c", p=128), **A))
        insts.append(eng.dma_start(out=b3v[:], in_=b3.rearrange("(c p) -> p c", p=128), **A))
        insts.append(eng.dma_start(out=gb1v[:], in_=gb1.rearrange("(c p) -> p c", p=128), **A))
        insts.append(eng.dma_start(out=gb2v[:], in_=gb2.rearrange("(c p) -> p c", p=128), **A))
        insts.append(eng.dma_start(out=gvec[:], in_=tsg.rearrange("(c p) -> p c", p=128), **A))
        insts.append(eng.dma_start(out=tdv[0:64, 0:1], in_=td.rearrange("(p o) -> p o", o=1), **A))
        insts.append(eng.dma_start(out=tdv[64:128, 0:1], in_=tf.rearrange("(p o) -> p o", o=1), **A))
        insts.append(eng.dma_start(out=misc[0:1, 0:1], in_=b4.rearrange("(o q) -> o q", q=1), **A))
        insts.append(eng.dma_start(out=misc[0:1, 1:2], in_=gb3.rearrange("(o q) -> o q", q=1), **A))
        insts.append(eng.dma_start(out=misc[0:1, 2:3], in_=arr.rearrange("(o q) -> o q", q=1), **A))
        return insts

    sch.add("sync", em_pvec, inc="LDV", n=13, amt=16, key="pvec",
            writes=("lnwv", "lnbv", "b1v", "b2v", "b3v", "gb1v", "gb2v",
                    "gvec", "tdv", "miscin"))

    for ej in (0, 1):
        def em_eax(eng, ej=ej):
            srcx = x_fm[0:1, :, ej * T:(ej + 1) * T].rearrange(
                "one (dc p) t -> p (one dc) t", p=128)
            return [eng.dma_start(
                out=XL[ej][:, 4 * q * T // 2:4 * (q + 1) * T // 2].rearrange(
                    "p (dc t) -> p dc t", dc=2),
                in_=srcx[:, 2 * q:2 * q + 2, :]) for q in (0, 1)]
        sch.add("sync", em_eax, inc=f"LD{ej}", n=2, amt=16, key=f"b0_a_x{ej}a",
                writes=(f"XL{ej}a",))

        def em_eax2(eng, ej=ej):
            srcx = x_fm[0:1, :, ej * T:(ej + 1) * T].rearrange(
                "one (dc p) t -> p (one dc) t", p=128)
            return [eng.dma_start(
                out=XL[ej][:, 4 * q * T // 2:4 * (q + 1) * T // 2].rearrange(
                    "p (dc t) -> p dc t", dc=2),
                in_=srcx[:, 2 * q:2 * q + 2, :]) for q in (2, 3)]
        sch.add("gpsimd", em_eax2, inc=f"LG{ej}", n=2, amt=16, key=f"b0_a_x{ej}",
                writes=(f"XL{ej}b",))

    sch.add("sync", lambda e: dma_w_to_stage(e, w1, DM, DM, 0),
            inc="LDS", n=1, amt=16, key="pw1", writes=("stage",))
    sch.add("sync", lambda e: dma_w_to_stage(e, w2, DM, 512, 8192),
            inc="LDS", n=1, amt=16, key="pw2", writes=("stage2",))
    sch.add("vector", lambda e: e.tensor_copy(W1s[:], xnF[:, 0:8192]),
            inc="VE", key="cw1", reads=("stage",), writes=("W1s",))
    sch.add("vector", lambda e: e.tensor_copy(W2s[:], xnF[:, 8192:12288]),
            inc="VE", key="cw2", reads=("stage2",), writes=("W2s",))
    sch.add("sync", lambda e: dma_w_to_stage(e, w3, 512, 256, 0),
            inc="LDS", n=1, amt=16, key="pw3", writes=("stage",))
    sch.add("sync", lambda e: dma_w_to_stage(e, w4, 256, 1, 8192),
            inc="LDS", n=1, amt=16, key="pw4", writes=("stage2",))
    sch.add("vector", lambda e: e.tensor_copy(W3s[:], xnF[:, 0:1024]),
            inc="VE", key="cw3", reads=("stage",), writes=("W3s",))
    sch.add("vector", lambda e: e.tensor_copy(W4s[:], xnF[:, 8192:8194]),
            inc="VE", key="cw4", reads=("stage2",), writes=("W4s",))
    sch.add("sync", lambda e: dma_w_to_stage(e, kvT, DM, 128, 0),
            inc="LDS", n=1, amt=16, key="pkv", writes=("stage",))
    sch.add("vector", lambda e: e.tensor_copy(KV0[:, 0:1024], xnF[:, 0:1024]),
            inc="VE", key="ckv0", reads=("stage",), writes=("H1",))

    def em_kvscale(eng):
        last = None
        for ci in range(8):
            last = eng.tensor_scalar_mul(
                KVs[:, ci * 128:(ci + 1) * 128],
                xnF[:, ci * 128:(ci + 1) * 128], lnwv[:, ci:ci + 1])
        return last
    sch.add("vector", em_kvscale, inc="VE", key="ckv",
            reads=("stage", "lnwv"), writes=("KVs",))

    sch.add("sync", lambda e: [e.dma_start(out=xnF[0:64, 0:1024], in_=owT)],
            inc="LDS", n=1, amt=16, key="pow", writes=("stage",))
    sch.add("vector", lambda e: e.tensor_copy(OWs[:], xnF[0:64, 0:1024]),
            inc="VE", key="cow", reads=("stage",), writes=("OWs",))
    sch.add("sync", lambda e: dma_w_to_stage(e, gw3, 256, 1, 0),
            inc="LDS", n=1, amt=16, key="pg3", writes=("stage",))
    sch.add("vector", lambda e: e.tensor_copy(G3s[:], xnF[:, 0:2]),
            inc="VE", key="cg3", reads=("stage",), writes=("G3s",))

    sch.add("vector", lambda e: e.tensor_copy(lnbB[:], lnbv[:]),
            inc="VE", key="clnb", reads=("lnbv",), writes=("lnbB",))

    def em_memsets(eng):
        eng.memset(onesb[:], 1.0)
        eng.memset(onesf[:], 1.0)
        return eng.memset(misc[0:1, 8:9], EPS)
    sch.add("vector", em_memsets, inc="VE", key="pmemset",
            writes=("onesb", "onesf", "epsc"))

    def em_umm(eng):
        insts = []
        for ci in range(8):
            insts.append(eng.matmul(bankrow(4, 0)[:, 0:128], onesb[:],
                                    KVs[:, ci * 128:(ci + 1) * 128],
                                    start=(ci == 0), stop=(ci == 7)))
        return insts[-1]
    sch.add("tensor", em_umm, inc="TE", key="t_u",
            reads=("KVs", "onesb"), writes=("pb4",))

    def em_cmm(eng):
        insts = []
        for ci in range(8):
            insts.append(eng.matmul(bankrow(5, 0)[:, 0:128], lnbB[:, ci:ci + 1],
                                    KV0[:, ci * 128:(ci + 1) * 128],
                                    start=(ci == 0), stop=(ci == 7)))
        return insts[-1]
    sch.add("tensor", em_cmm, inc="TE", key="t_c",
            reads=("H1", "lnbB"), writes=("pb5",))

    def em_evac_uc(eng):
        eng.tensor_copy(stv[0:1, 0:128], bankrow(4, 0)[:, 0:128])
        return eng.tensor_copy(stv[0:1, 128:256], bankrow(5, 0)[:, 0:128])
    sch.add("vector", em_evac_uc, inc="VE", key="e_uc",
            reads=("pb4", "pb5"), writes=("stv0",))

    sch.add("gpsimd", lambda e: [e.dma_start(out=ucb.rearrange("(o n) -> o n", o=1),
                                             in_=stv[0:1, 0:256])],
            inc="GPU", n=1, amt=16, key="m_uc1", reads=("stv0",))

    def em_dma_uc(eng):
        i1 = eng.dma_start(out=uvec[:], in_=ucb[0:128].rearrange("(p o) -> p o", o=1),
                           allow_slow_non_contiguous=True)
        i2 = eng.dma_start(out=cvec[:], in_=ucb[128:256].rearrange("(p o) -> p o", o=1),
                           allow_slow_non_contiguous=True)
        return [i1, i2]
    sch.add("gpsimd", em_dma_uc, waits=("m_uc1",), inc="GPU2", n=2, amt=16, key="m_uc",
            writes=("uvec", "cvec"))
    sch.add("gpsimd", lambda e: e.dma_start(out=urow[:],
                                            in_=ucb[0:128].rearrange("(o n) -> o n", o=1)),
            waits=("m_uc1",), inc="GPU3", amt=16, key="m_ur", writes=("urow",))

    sch.add("scalar", lambda e: e.activation(evec[:], tdv[:], Act.Exp),
            inc="SE", key="s_ev", reads=("tdv",), writes=("evec",))
    sch.add("vector", lambda e: e.tensor_scalar_mul(negef[64:128, :], evec[64:128, :], -1.0),
            inc="VE", key="v_negef", reads=("evec",), writes=("negef",))
    sch.add("vector", lambda e: e.tensor_mul(biask[64:128, :], negef[64:128, :], cvec[64:128, :]),
            inc="VE", key="v_biask", reads=("negef", "cvec"), writes=("biask",))

    sch.add("scalar", lambda e: e.activation(gtmp[:], gvec[:], Act.Exp, scale=-1.0),
            inc="SE", key="s_gex", reads=("gvec",), writes=("gtmp",))
    sch.add("vector", lambda e: e.tensor_scalar_add(gtmp[:], gtmp[:], 1.0),
            inc="VE", key="v_g1", reads=("gtmp",), writes=("gtmp",))
    sch.add("vector", lambda e: e.reciprocal(gvec[:], gtmp[:]),
            inc="VE", key="v_g2", reads=("gtmp",), writes=("gvec",))
    sch.add("vector", lambda e: e.tensor_scalar(omgv[:], gvec[:], -1.0, 1.0, Alu.mult, Alu.add),
            inc="VE", key="v_gates", reads=("gvec",), writes=("omgv",))

    def em_misc(eng):
        eng.tensor_scalar_mul(misc[0:1, 0:1], misc[0:1, 0:1], 0.5)
        return eng.tensor_scalar_mul(misc[0:1, 1:2], misc[0:1, 1:2], 0.5)
    sch.add("vector", em_misc, inc="VE", key="v_misc",
            reads=("miscin",), writes=("misc05",))

    def warmers(tag, n=2, wait_key=None):
        def em_w(eng):
            insts = []
            for i in range(n):
                insts.append(eng.matmul(bankrow(0, 0), onesb[:], SQ[:, 0:512],
                                        start=True, stop=True))
            return insts[-1]
        sch.add("tensor", em_w, waits=(wait_key,) if wait_key else (),
                inc="TE", key=tag)


    def phaseA(b):
        P = f"b{b}_"
        sch.add("vector", lambda e: e.memset(GACC[:], 0.0),
                inc="VE", key=P + "g0", writes=("GACC",))
        for j in range(NT):
            xl = XL[j % 2]
            rXL = f"XL{j % 2}"
            p = j % 2
            sA = 4 + p
            bMu = 6 + p

            def em_ax(eng, b=b, j=j, xl=xl):
                src = x_fm[b:b + 1, :, j * T:(j + 1) * T].rearrange(
                    "one (dc p) t -> p (one dc) t", p=128)
                return [eng.dma_start(
                    out=xl[:, 4 * q * T // 2:4 * (q + 1) * T // 2].rearrange(
                        "p (dc t) -> p dc t", dc=2),
                    in_=src[:, 2 * q:2 * q + 2, :]) for q in (0, 1)]
            if not (b == 0 and j <= 1):
                sch.add("sync", em_ax, inc=f"LD{p}", n=2, amt=16, key=P + f"a_x{j}a",
                        writes=(rXL + "a",))

            def em_ax2(eng, b=b, j=j, xl=xl):
                src = x_fm[b:b + 1, :, j * T:(j + 1) * T].rearrange(
                    "one (dc p) t -> p (one dc) t", p=128)
                return [eng.dma_start(
                    out=xl[:, 4 * q * T // 2:4 * (q + 1) * T // 2].rearrange(
                        "p (dc t) -> p dc t", dc=2),
                    in_=src[:, 2 * q:2 * q + 2, :]) for q in (2, 3)]
            if not (b == 0 and j <= 1):
                sch.add("gpsimd", em_ax2, inc=f"LG{p}", n=2, amt=16, key=P + f"a_x{j}",
                        writes=(rXL + "b",))

            xbb = XB if j % 2 == 0 else TTb
            sqb = SQ if j % 2 == 0 else H1
            rXB = "XB" if j % 2 == 0 else "TTb"
            rSQ = "SQ" if j % 2 == 0 else "H1"
            sch.add("scalar", lambda e, xl=xl, sqb=sqb: e.activation(sqb[:], xl[:], Act.Square),
                    inc="SE", key=P + f"a_sq{j}", reads=(rXL + "a", rXL + "b"), writes=(rSQ,))
            sch.add("scalar", lambda e, xl=xl, xbb=xbb: e.activation(xbb[:], xl[:], Act.Copy),
                    inc="SE", key=P + f"a_xb{j}", reads=(rXL + "a", rXL + "b"), writes=(rXB,))

            def em_amm(eng, sA=sA, xbb=xbb, sqb=sqb):
                insts = []
                for c in range(DC):
                    insts.append(eng.matmul(bankrow(sA, 0), onesb[:],
                                            xbb[:, c * T:(c + 1) * T],
                                            start=(c == 0), stop=(c == DC - 1)))
                for c in range(DC):
                    insts.append(eng.matmul(bankrow(sA, 32), onesb[:],
                                            sqb[:, c * T:(c + 1) * T],
                                            start=(c == 0), stop=(c == DC - 1)))
                return insts[-1]
            sch.add("tensor", em_amm, inc="TE", key=P + f"a_mm{j}",
                    reads=(rXB, rSQ, "onesb"), writes=(f"pb{sA}",))

            def em_afv(eng, sA=sA, p=p):
                eng.tensor_scalar_mul(stslot(p, 0), bankrow(sA, 0), 1.0 / DM)
                eng.tensor_scalar_mul(stslot(p, 1), bankrow(sA, 32), 1.0 / DM)
                eng.tensor_mul(stslot(p, 2), stslot(p, 0), stslot(p, 0))
                return eng.tensor_sub(stslot(p, 1), stslot(p, 1), stslot(p, 2))
            sch.add("vector", em_afv, inc="VE", key=P + f"a_fv{j}",
                    reads=(f"pb{sA}",), writes=("stv0", "muA"))

            def em_afs(eng, p=p):
                eng.activation(stslot(p, 2), stslot(p, 1), Act.Ln, bias=misc[0:1, 8:9])
                return eng.activation(stslot(p, 1), stslot(p, 2), Act.Exp, scale=-0.5)
            sch.add("scalar", em_afs, inc="SE", key=P + f"a_fs{j}",
                    reads=("stv0", "epsc"), writes=("stv0",))

            sch.add("tensor", lambda e, bMu=bMu, p=p: e.matmul(
                        bank(bMu), onesf[:], stslot(p, 0), start=True, stop=True),
                    inc="TE", key=P + f"a_bmu{j}",
                    reads=("muA", "onesf"), writes=(f"pb{bMu}",))

            def em_ax1(eng, j=j, xl=xl, bMu=bMu):
                mu_b = bank(bMu).unsqueeze(1).broadcast_to([128, DC, T])
                xlv = xl[:].rearrange("p (dc t) -> p dc t", dc=DC)
                dst = xntile(j).rearrange("p (dc t) -> p dc t", dc=DC)
                return eng.scalar_tensor_tensor(dst, mu_b, -1.0, xlv, Alu.mult, Alu.add)
            sch.add("vector", em_ax1, inc="VE", key=P + f"a_x1{j}",
                    reads=(f"pb{bMu}", rXL + "a", rXL + "b"), writes=(f"xn{j}",))

            sch.add("tensor", lambda e, sA=sA, p=p: e.matmul(
                        bank(sA), onesf[:], stslot(p, 1), start=True, stop=True),
                    inc="TE", key=P + f"a_br{j}",
                    reads=("stv0", "onesf"), writes=(f"pb{sA}",))

            def em_axn(eng, j=j, sA=sA):
                last = None
                for c in range(DC):
                    last = eng.scalar_tensor_tensor(xnchunk(j, c), xnchunk(j, c), 1.0,
                                                    bank(sA), Alu.mult, Alu.mult,
                                                    accum_out=ACCS[:, c:c + 1])
                return last
            sch.add("vector", em_axn, inc="VE", key=P + f"a_xn{j}",
                    reads=(f"pb{sA}",), writes=(f"xn{j}", "ACCS"))
            sch.add("vector", lambda e: e.tensor_add(GACC[:], GACC[:], ACCS[:]),
                    inc="VE", key=P + f"a_ga{j}", reads=("ACCS",), writes=("GACC",))
            warmers(P + f"a_w{j}a", 3, P + f"a_sq{j}")
            warmers(P + f"a_w{j}b", 3, P + f"a_sq{j}")
            yield

    def phaseB(b):
        P = f"b{b}_"

        def tilevars(j):
            p = j % 2
            return dict(
                j=j, jj=(j + SHIFT_TILES) % NT, p=p, s2=j % 4, bA=4 + j % 4,
                xsb=XB if p == 0 else TTb,
                sqb=SQ if p == 0 else H1,
                rXS="XB" if p == 0 else "TTb",
                rSQ="SQ" if p == 0 else "H1",
                tmpb=TMP[:] if p == 0 else SSBf[:, 0:T],
                p1b=P1B[:] if p == 0 else SSBf[:, T:2 * T],
                esb=ESB[:] if p == 0 else SSBf[:, 2 * T:3 * T],
                rTMP=("TMP",) if p == 0 else ("TMP1", "SSB"),
                rP1=("P1B",) if p == 0 else ("P1B1", "SSB"),
                rES=("ESB",) if p == 0 else ("ESB1", "SSB"),
            )

        def stage1(v):
            j, jj, xsb, sqb = v["j"], v["jj"], v["xsb"], v["sqb"]
            bA, s2, rXS, rSQ = v["bA"], v["s2"], v["rXS"], v["rSQ"]

            def em_bxs(eng, j=j, jj=jj, xsb=xsb):
                eng.tensor_sub(xsb[:], xntile(jj), xntile(j))
                for c in range(DC):
                    eng.tensor_scalar_mul(xsb[:, c * T:(c + 1) * T],
                                          xsb[:, c * T:(c + 1) * T],
                                          gvec[:, c:c + 1])
                return eng.tensor_add(xsb[:], xsb[:], xntile(j))
            sch.add("vector", em_bxs, inc="VE", key=P + f"b_xs{j}",
                    reads=(f"xn{j}", f"xn{jj}", "gvec"), writes=(rXS,))

            sch.add("scalar", lambda e, xsb=xsb, sqb=sqb: e.activation(
                        sqb[:], xsb[:], Act.Square),
                    inc="SE", key=P + f"b_sq{j}", reads=(rXS,), writes=(rSQ,))

            def em_bmm(eng, bA=bA, s2=s2, xsb=xsb, sqb=sqb):
                insts = []
                for c in range(DC):
                    insts.append(eng.matmul(bank(bA), KVs[:, c * 128:(c + 1) * 128],
                                            xsb[:, c * T:(c + 1) * T],
                                            start=(c == 0), stop=False))
                for c in range(DC):
                    insts.append(eng.matmul(bankrow(s2, 0), onesb[:],
                                            xsb[:, c * T:(c + 1) * T],
                                            start=(c == 0), stop=(c == DC - 1)))
                for c in range(DC):
                    insts.append(eng.matmul(bankrow(s2, 32), onesb[:],
                                            sqb[:, c * T:(c + 1) * T],
                                            start=(c == 0), stop=(c == DC - 1)))
                return insts[-1]
            sch.add("tensor", em_bmm, inc="TE", key=P + f"b_mm{j}",
                    reads=(rXS, rSQ, "KVs", "onesb"), writes=(f"pb{bA}", f"pb{s2}"))

        def stage2a(v):
            j, p, s2, bA = v["j"], v["p"], v["s2"], v["bA"]

            def em_bfv(eng, s2=s2, p=p):
                eng.tensor_scalar_mul(stslot(p, 0), bankrow(s2, 0), -1.0 / DM)
                eng.tensor_scalar_mul(stslot(p, 1), bankrow(s2, 32), 1.0 / DM)
                eng.tensor_mul(stslot(p, 2), stslot(p, 0), stslot(p, 0))
                return eng.tensor_sub(stslot(p, 1), stslot(p, 1), stslot(p, 2))
            sch.add("vector", em_bfv, inc="VE", key=P + f"b_fv{j}",
                    reads=(f"pb{s2}",), writes=(f"stv{p}",))

            def em_bfs(eng, p=p):
                eng.activation(stslot(p, 2), stslot(p, 1), Act.Ln, bias=misc[0:1, 8:9])
                return eng.activation(stslot(p, 1), stslot(p, 2), Act.Exp, scale=-0.5)
            sch.add("scalar", em_bfs, inc="SE", key=P + f"b_fs{j}",
                    reads=(f"stv{p}", "epsc"), writes=(f"stv{p}",))

            def em_bbc(eng, bA=bA, s2=s2, p=p):
                i1 = eng.matmul(bank(bA), urow[:], stslot(p, 0), start=False, stop=True)
                i2 = eng.matmul(bank(s2), onesf[:], stslot(p, 1), start=True, stop=True)
                return [i1, i2]
            sch.add("tensor", em_bbc, inc="TE", key=P + f"b_bc{j}", n=2,
                    reads=(f"stv{p}", "onesf", "urow"), writes=(f"pb{bA}", f"pb{s2}"))

        def stage2b(v):
            j, s2, p1b, rP1 = v["j"], v["s2"], v["p1b"], v["rP1"]
            sch.add("vector", lambda e, s2=s2, p1b=p1b: e.tensor_copy(p1b, bank(s2)),
                    inc="VE", key=P + f"b_rs{j}", reads=(f"pb{s2}",), writes=rP1)

        def stage3a(v):
            j, bA, tmpb, p1b, esb = v["j"], v["bA"], v["tmpb"], v["p1b"], v["esb"]
            rTMP, rP1, rES = v["rTMP"], v["rP1"], v["rES"]

            sch.add("vector", lambda e, bA=bA, tmpb=tmpb, p1b=p1b: e.tensor_mul(
                        tmpb, bank(bA), p1b),
                    inc="VE", key=P + f"b_tmp{j}",
                    reads=(f"pb{bA}",) + rP1, writes=rTMP)

            sch.add("scalar", lambda e, tmpb=tmpb, esb=esb: e.activation(
                        esb[64:128, :], tmpb[64:128, :], Act.Exp,
                        bias=biask[64:128, 0:1], scale=negef[64:128, 0:1]),
                    inc="SE", key=P + f"b_exp{j}",
                    reads=rTMP + ("biask", "negef"), writes=rES)

            sch.add("scalar", lambda e, esb=esb: e.dma_start(out=esb[0:64, :],
                                                             in_=esb[64:128, :]),
                    inc="GPE", amt=16, key=P + f"b_e2{j}",
                    reads=rES, writes=rES)

        def stage3b(v):
            j, tmpb, esb = v["j"], v["tmpb"], v["esb"]
            rTMP, rES = v["rTMP"], v["rES"]

            def em_bwkv(eng, j=j, tmpb=tmpb, esb=esb):
                return eng.scalar_tensor_tensor(
                    XL[0][0:64, j * T:(j + 1) * T], tmpb[0:64, :], cvec[0:64, 0:1],
                    esb[0:64, :], Alu.add, Alu.mult)
            sch.add("vector", em_bwkv, inc="VE", key=P + f"b_wkv{j}",
                    reads=rTMP + rES + ("cvec",), writes=("XL0a", "XL0b"))

        vs = [tilevars(j) for j in range(NT)]
        for r in range(NT + 4):
            if r < NT:
                stage1(vs[r])
            if 0 <= r - 2 < NT:
                stage2a(vs[r - 2])
            if 0 <= r - 3 < NT:
                stage2b(vs[r - 3])
                stage3a(vs[r - 3])
            if 0 <= r - 4 < NT:
                stage3b(vs[r - 4])
            yield

    def phaseB_scan(b):
        P = f"b{b}_"

        def em_scan(eng, b=b):
            d0 = evec[0:64, 0:1].broadcast_to([64, S])
            return eng.tensor_tensor_scan(Y[b][0:64, :], d0, XL[0][0:64, :], 0.0,
                                          Alu.mult, Alu.add)
        sch.add("vector", em_scan, inc="VE", key=P + "scan",
                reads=("XL0a", "XL0b", "evec"), writes=(f"Y{b}",))

    def phase_gf(b):
        P = f"b{b}_"
        a_sl = slice(4, 5) if b == 0 else slice(10, 11)
        b_sl = slice(3, 4) if b == 0 else slice(9, 10)

        sch.add("vector", lambda e: e.tensor_scalar_mul(GBF[:], GACC[:], 1.0 / S),
                inc="VE", key=P + "g_gbf", reads=("GACC",), writes=("GBF",))

        def dma_gstage(eng, w, nr):
            src = w.rearrange("(ci p) ck -> p ci ck", p=128)
            dst = XL[1][:, 0:nr * w.shape[1]].rearrange("p (ci ck) -> p ci ck", ci=nr)
            return eng.dma_start(out=dst, in_=src)

        sch.add("sync", lambda e: dma_gstage(e, gw1, 8),
                inc="LD1", n=1, amt=16, key=P + "g_l1", writes=("XL1a", "XL1b"))
        sch.add("vector", lambda e: e.tensor_copy(H1[:, 0:4096], XL[1][:, 0:4096]),
                inc="VE", key=P + "g_c1", reads=("XL1a", "XL1b"), writes=("H1",))
        yield

        def em_gm1(eng):
            insts = []
            for co in range(4):
                for ci in range(8):
                    insts.append(eng.matmul(
                        bank(4)[:, co:co + 1],
                        H1[:, (ci * 4 + co) * 128:(ci * 4 + co + 1) * 128],
                        GBF[:, ci:ci + 1], start=(ci == 0), stop=(ci == 7)))
            return insts[-1]
        sch.add("tensor", em_gm1, inc="TE", key=P + "g_m1",
                reads=("H1", "GBF"), writes=("pb4",))

        def em_ge1(eng):
            last = None
            for co in range(4):
                last = eng.activation(HG1[:, co:co + 1], bank(4)[:, co:co + 1],
                                      Act.Gelu, bias=gb1v[:, co:co + 1])
            return last
        sch.add("scalar", em_ge1, inc="SE", key=P + "g_e1",
                reads=("pb4", "gb1v"), writes=("HG1",))

        sch.add("sync", lambda e: dma_gstage(e, gw2, 4),
                inc="LD1", n=1, amt=16, key=P + "g_l2", writes=("XL1a", "XL1b"))
        sch.add("vector", lambda e: e.tensor_copy(H2[:, 0:1024], XL[1][:, 0:1024]),
                inc="VE", key=P + "g_c2", reads=("XL1a", "XL1b"), writes=("H2",))
        yield

        def em_gm2(eng):
            insts = []
            for co in range(2):
                for ci in range(4):
                    insts.append(eng.matmul(
                        bank(5)[:, co:co + 1],
                        H2[:, (ci * 2 + co) * 128:(ci * 2 + co + 1) * 128],
                        HG1[:, ci:ci + 1], start=(ci == 0), stop=(ci == 3)))
            return insts[-1]
        sch.add("tensor", em_gm2, inc="TE", key=P + "g_m2",
                reads=("H2", "HG1"), writes=("pb5",))

        def em_ge2(eng):
            last = None
            for co in range(2):
                last = eng.activation(HG2[:, co:co + 1], bank(5)[:, co:co + 1],
                                      Act.Gelu, bias=gb2v[:, co:co + 1])
            return last
        sch.add("scalar", em_ge2, inc="SE", key=P + "g_e2",
                reads=("pb5", "gb2v"), writes=("HG2",))

        def em_gm3(eng):
            eng.matmul(bankrow(6, 0)[:, 0:1], G3s[:, 0:1], HG2[:, 0:1],
                       start=True, stop=False)
            return eng.matmul(bankrow(6, 0)[:, 0:1], G3s[:, 1:2], HG2[:, 1:2],
                              start=False, stop=True)
        sch.add("tensor", em_gm3, inc="TE", key=P + "g_m3",
                reads=("HG2", "G3s"), writes=("pb6",))

        sch.add("scalar", lambda e: e.activation(misc[0:1, 6:7], bankrow(6, 0)[:, 0:1],
                                                 Act.Tanh, bias=misc[0:1, 1:2], scale=0.5),
                inc="SE", key=P + "g_thg", reads=("pb6", "misc05"), writes=("miscg",))

        def em_gab1(eng, b_sl=b_sl):
            eng.tensor_scalar(misc[0:1, 5:6], misc[0:1, 6:7], 0.5, 0.5, Alu.mult, Alu.add)
            eng.tensor_scalar_mul(misc[0:1, b_sl], misc[0:1, 2:3], 0.3)
            return eng.tensor_scalar(misc[0:1, 7:8], misc[0:1, 2:3], -0.7, 0.1,
                                     Alu.mult, Alu.add)
        sch.add("vector", em_gab1, inc="VE", key=P + "g_ab1",
                reads=("miscg", "miscin"), writes=("miscg2",))
        sch.add("vector", lambda e, a_sl=a_sl: e.tensor_mul(
                    misc[0:1, a_sl], misc[0:1, 2:3], misc[0:1, 5:6]),
                inc="VE", key=P + "g_ab2", reads=("miscg2",), writes=("miscg3",))
        sch.add("vector", lambda e, a_sl=a_sl: e.tensor_scalar(
                    misc[0:1, a_sl], misc[0:1, a_sl], 1.4, misc[0:1, 7:8],
                    Alu.mult, Alu.add),
                inc="VE", key=P + "g_ab", reads=("miscg3",), writes=(f"miscab{b}",))
        yield

    def chain_pair(b, cyc, j0):
        P = f"b{b}_"
        a_sl = slice(4, 5) if b == 0 else slice(10, 11)
        b_sl = slice(3, 4) if b == 0 else slice(9, 10)

        H1b = [H1, XB]
        H2b = [H2, SQ[:].rearrange("p (a t) -> p a t", a=4)[:, 0:4, :].rearrange("p a t -> p (a t)") if False else H2, ]
        def h1buf(s):
            return H1 if s == 0 else XB
        def h2buf(s):
            return H2 if s == 0 else SQ[:, 0:4 * T]
        def h3buf(s):
            return H3 if s == 0 else SQ[:, 4 * T:6 * T]
        def h1res(s):
            return "H1" if s == 0 else "XB"
        def h2res(s):
            return "H2" if s == 0 else "SQ"
        def h3res(s):
            return "H3" if s == 0 else "SQ"

        def K(s):
            return P + f"c{cyc}_{j0 + s}_"

        pi = (j0 // 2) % 2
        if pi == 0:
            thbufs = [thv[:], cplv[:]]
            thres = ["thv", "cplv2"]
        else:
            thbufs = [stv[0:1, 0:T], stv[0:1, T:2 * T]]
            thres = ["stv0", "stv0"]

        def em_h1r(eng, j, bb, cos):
            insts = []
            for ii, co in enumerate(cos):
                for ci in range(DC):
                    insts.append(eng.matmul(
                        bank(bb + ii),
                        W1s[:, (ci * 8 + co) * 128:(ci * 8 + co + 1) * 128],
                        xnchunk(j, ci), start=(ci == 0), stop=(ci == DC - 1)))
            return insts[-1]

        def em_e1r(eng, s, bb, cos):
            last = None
            for ii, co in enumerate(cos):
                last = eng.activation(h1buf(s)[:, co * T:(co + 1) * T],
                                      bank(bb + ii), Act.Gelu,
                                      bias=b1v[:, co:co + 1])
            return last

        pbs = lambda bb: tuple(f"pb{bb + k}" for k in range(4))

        for rnd, cos in ((0, (0, 1, 2, 3)), (1, (4, 5, 6, 7))):
            for s in (0, 1):
                j = j0 + s
                bb = 4 * s
                sch.add("tensor", lambda e, j=j, bb=bb, cos=cos: em_h1r(e, j, bb, cos),
                        inc="TE", key=K(s) + f"h1{rnd}",
                        reads=(f"xn{j}", "W1s"), writes=pbs(bb))
                sch.add("scalar", lambda e, s=s, bb=bb, cos=cos: em_e1r(e, s, bb, cos),
                        inc="SE", key=K(s) + f"e1{rnd}",
                        reads=pbs(bb) + ("b1v",), writes=(h1res(s),))

        for s in (0, 1):
            bb = 4 * s

            def em_h2(eng, s=s, bb=bb):
                insts = []
                for co in range(4):
                    for ci in range(DC):
                        insts.append(eng.matmul(
                            bank(bb + co),
                            W2s[:, (ci * 4 + co) * 128:(ci * 4 + co + 1) * 128],
                            h1buf(s)[:, ci * T:(ci + 1) * T],
                            start=(ci == 0), stop=(ci == DC - 1)))
                return insts[-1]
            sch.add("tensor", em_h2, inc="TE", key=K(s) + "h2",
                    reads=(h1res(s), "W2s"), writes=pbs(bb))

            def em_e2c(eng, s=s, bb=bb):
                last = None
                for co in range(4):
                    last = eng.activation(h2buf(s)[:, co * T:(co + 1) * T],
                                          bank(bb + co), Act.Gelu,
                                          bias=b2v[:, co:co + 1])
                return last
            sch.add("scalar", em_e2c, inc="SE", key=K(s) + "e2",
                    reads=pbs(bb) + ("b2v",), writes=(h2res(s),))

        for s in (0, 1):
            bb = 4 * s

            def em_h3(eng, s=s, bb=bb):
                insts = []
                for co in range(2):
                    for ci in range(4):
                        insts.append(eng.matmul(
                            bank(bb + co),
                            W3s[:, (ci * 2 + co) * 128:(ci * 2 + co + 1) * 128],
                            h2buf(s)[:, ci * T:(ci + 1) * T],
                            start=(ci == 0), stop=(ci == 3)))
                return insts[-1]
            sch.add("tensor", em_h3, inc="TE", key=K(s) + "h3",
                    reads=(h2res(s), "W3s"), writes=(f"pb{bb}", f"pb{bb + 1}"))

            def em_e3(eng, s=s, bb=bb):
                last = None
                for co in range(2):
                    last = eng.activation(h3buf(s)[:, co * T:(co + 1) * T],
                                          bank(bb + co), Act.Gelu,
                                          bias=b3v[:, co:co + 1])
                return last
            sch.add("scalar", em_e3, inc="SE", key=K(s) + "e3",
                    reads=(f"pb{bb}", f"pb{bb + 1}", "b3v"), writes=(h3res(s),))

        for s in (0, 1):
            bb = 4 * s

            def em_h4(eng, s=s, bb=bb):
                eng.matmul(bankrow(bb + 2, 0), W4s[:, 0:1], h3buf(s)[:, 0:T],
                           start=True, stop=False)
                return eng.matmul(bankrow(bb + 2, 0), W4s[:, 1:2],
                                  h3buf(s)[:, T:2 * T], start=False, stop=True)
            sch.add("tensor", em_h4, inc="TE", key=K(s) + "h4",
                    reads=(h3res(s), "W4s"), writes=(f"pb{bb + 2}",))

            sch.add("scalar", lambda e, s=s, bb=bb: e.activation(
                        thbufs[s], bankrow(bb + 2, 0), Act.Tanh,
                        bias=misc[0:1, 0:1], scale=0.5),
                    inc="SE", key=K(s) + "th",
                    reads=(f"pb{bb + 2}", "misc05"),
                    writes=(thres[s], "stv1"))

        def tail():
            for s in (0, 1):
                bb = 4 * s
                thbuf = thbufs[s]

                sch.add("vector", lambda e, thbuf=thbuf: e.tensor_scalar(
                            thbuf, thbuf, misc[0:1, b_sl], misc[0:1, a_sl],
                            Alu.mult, Alu.add),
                        inc="VE", key=K(s) + "cpl",
                        reads=(thres[s], f"miscab{b}"), writes=(thres[s],))

                sch.add("tensor", lambda e, bb=bb, thbuf=thbuf: e.matmul(
                            bank(bb + 3), onesf[:], thbuf, start=True, stop=True),
                        inc="TE", key=K(s) + "cbc",
                        reads=(thres[s], "onesf"), writes=(f"pb{bb + 3}",))

                sch.add("scalar", lambda e, s=s, bb=bb: e.activation(
                            CBs[:, s * T:(s + 1) * T], bank(bb + 3), Act.Copy),
                        inc="SE", key=K(s) + "cbe",
                        reads=(f"pb{bb + 3}",), writes=(f"CB{s}",))

            for s in (0, 1):
                j = j0 + s

                def em_um(eng, j=j, s=s):
                    base = j * DC * T
                    cb = CBs[:, s * T:(s + 1) * T].unsqueeze(1).broadcast_to([128, 4, T])
                    eng.tensor_tensor(
                        TTb[:, 0:4 * T].rearrange("p (a t) -> p a t", a=4),
                        xn[:, base + 4 * T:base + 8 * T].rearrange("p (a t) -> p a t", a=4),
                        cb, Alu.mult)
                    return eng.tensor_tensor(
                        TTb[:, 4 * T:8 * T].rearrange("p (a t) -> p a t", a=4),
                        xn[:, base:base + 4 * T].rearrange("p (a t) -> p a t", a=4),
                        cb, Alu.mult)
                sch.add("vector", em_um, inc="VE", key=K(s) + "um",
                        reads=(f"CB{s}", f"xn{j}"), writes=("TTb",))

                def em_upd(eng, j=j):
                    base = j * DC * T
                    eng.tensor_add(xn[:, base:base + 2 * T],
                                   xn[:, base:base + 2 * T], TTb[:, 0:2 * T])
                    eng.tensor_sub(xn[:, base + 2 * T:base + 6 * T],
                                   xn[:, base + 2 * T:base + 6 * T], TTb[:, 2 * T:6 * T])
                    return eng.tensor_add(xn[:, base + 6 * T:base + 8 * T],
                                          xn[:, base + 6 * T:base + 8 * T],
                                          TTb[:, 6 * T:8 * T])
                sch.add("vector", em_upd, inc="VE", key=K(s) + "upd",
                        reads=("TTb",), writes=(f"xn{j}",))
        return tail

    def phaseD_load(b, j):
        P = f"b{b}_"
        K = P + f"d{j}_"
        xl = XL[j % 2]
        rXL = f"XL{j % 2}"

        def em_dx(eng, b=b, j=j, xl=xl):
            src = x_fm[b:b + 1, :, j * T:(j + 1) * T].rearrange(
                "one (dc p) t -> p (one dc) t", p=128)
            return [eng.dma_start(
                out=xl[:, 4 * q * T // 2:4 * (q + 1) * T // 2].rearrange(
                    "p (dc t) -> p dc t", dc=2),
                in_=src[:, 2 * q:2 * q + 2, :]) for q in (0, 1)]
        sch.add("sync", em_dx, inc=f"LD{j % 2}", n=2, amt=16, key=K + "xa",
                writes=(rXL + "a",))

        def em_dx2(eng, b=b, j=j, xl=xl):
            src = x_fm[b:b + 1, :, j * T:(j + 1) * T].rearrange(
                "one (dc p) t -> p (one dc) t", p=128)
            return [eng.dma_start(
                out=xl[:, 4 * q * T // 2:4 * (q + 1) * T // 2].rearrange(
                    "p (dc t) -> p dc t", dc=2),
                in_=src[:, 2 * q:2 * q + 2, :]) for q in (2, 3)]
        sch.add("gpsimd", em_dx2, inc=f"LG{j % 2}", n=2, amt=16, key=K + "x",
                writes=(rXL + "b",))

    def phaseD(b, j, split_tail=False):
        P = f"b{b}_"
        K = P + f"d{j}_"
        xl = XL[j % 2]
        rXL = f"XL{j % 2}"

        sch.add("scalar", lambda e, j=j: e.activation(xntile(j), xntile(j), Act.Silu),
                inc="SE", key=K + "g", reads=(f"xn{j}",), writes=(f"xn{j}",))

        for rnd in range(2):
            bb = 4 * rnd
            ps = PSA if rnd == 0 else PSB

            def em_dmm(eng, b=b, j=j, rnd=rnd, bb=bb):
                insts = []
                for c in range(4 * rnd, 4 * rnd + 4):
                    insts.append(eng.matmul(bank(bb + c % 4),
                                            OWs[:, c * 128:(c + 1) * 128],
                                            Y[b][0:64, j * T:(j + 1) * T],
                                            start=True, stop=True))
                return insts[-1]
            sch.add("tensor", em_dmm, inc="TE", key=K + f"mm{rnd}",
                    reads=(f"Y{b}", "OWs"),
                    writes=tuple(f"pb{bb + k}" for k in range(4)))

            def em_dmul(eng, j=j, rnd=rnd, ps=ps):
                base = j * DC * T + rnd * 4 * T
                eng.tensor_mul(SSB[:, rnd * 4 * T:rnd * 4 * T + 2 * T],
                               ps[:, 0:2 * T], xn[:, base:base + 2 * T])
                return eng.tensor_mul(SSB[:, rnd * 4 * T + 2 * T:rnd * 4 * T + 4 * T],
                                      ps[:, 2 * T:4 * T], xn[:, base + 2 * T:base + 4 * T])
            sch.add("vector", em_dmul, inc="VE", key=K + f"mul{rnd}",
                    waits=(K + "g",),
                    reads=tuple(f"pb{bb + k}" for k in range(4)) + (f"xn{j}",),
                    writes=("SSB",))

        def em_dst_half(eng, rnd, b=b, j=j, xl=xl):
            dst = y_fm[b:b + 1, :, j * T:(j + 1) * T].rearrange(
                "one (dc p) t -> p (one dc) t", p=128)
            lo = rnd * 4 * T
            return eng.dma_start(
                out=dst[:, 4 * rnd:4 * rnd + 4, :],
                in_=xl[:, lo:lo + 4 * T].rearrange("p (dc t) -> p dc t", dc=4))

        if split_tail:
            for rnd, half in ((0, "a"), (1, "b")):
                lo = rnd * 4 * T
                rH = rXL + half
                sch.add("vector", lambda e, xl=xl, lo=lo: e.tensor_add(
                            xl[:, lo:lo + 4 * T], SSB[:, lo:lo + 4 * T],
                            xl[:, lo:lo + 4 * T]),
                        inc="VE", key=K + "add" + half,
                        waits=(K + ("xa" if rnd == 0 else "x"),),
                        reads=("SSB", rH), writes=(rH,))
                sch.add("gpsimd", lambda e, rnd=rnd: em_dst_half(e, rnd),
                        inc=f"GPS{j % 2}", n=1, amt=16, key=K + "st" + half,
                        reads=(rH,))
        else:
            sch.add("vector", lambda e, xl=xl: e.tensor_add(xl[:], SSB[:], xl[:]),
                    inc="VE", key=K + "add", waits=(K + "x", K + "xa"),
                    reads=("SSB", rXL + "a", rXL + "b"),
                    writes=(rXL + "a", rXL + "b"))

            def em_dst(eng):
                return [em_dst_half(eng, 0), em_dst_half(eng, 1)]
            sch.add("gpsimd", em_dst, inc=f"GPS{j % 2}", n=2, amt=16, key=K + "st",
                    reads=(rXL + "a", rXL + "b"))

    for b in range(NB):
        for _ in phaseA(b):
            pass
        for _ in phaseB(b):
            pass
        for _ in phase_gf(b):
            pass
        phaseB_scan(b)
        pending_tail = None
        for cyc in range(3):
            if cyc == 1:
                phaseD_load(b, 0)
                phaseD_load(b, 1)
            for j0 in range(0, NT, 2):
                t = chain_pair(b, cyc, j0)
                if pending_tail is not None:
                    pending_tail()
                pending_tail = t
                if cyc == 2 and j0 >= 2:
                    phaseD(b, j0 - 2)
                    phaseD(b, j0 - 1)
                    phaseD_load(b, j0)
                    phaseD_load(b, j0 + 1)
        pending_tail()
        phaseD(b, NT - 2, split_tail=True)
        phaseD(b, NT - 1, split_tail=True)

    sch.add("gpsimd", lambda e: None,
            waits=(f"b1_d{NT-2}_sta", f"b1_d{NT-2}_stb",
                   f"b1_d{NT-1}_sta", f"b1_d{NT-1}_stb"))

    sch.finalize(nc, es)
    block = es.enter_context(nc.Block())
    block.sync(sch.emit_engine("sync"))
    block.gpsimd(sch.emit_engine("gpsimd"))
    block.scalar(sch.emit_engine("scalar"))
    block.vector(sch.emit_engine("vector"))
    block.tensor(sch.emit_engine("tensor"))
    es.close()
    return nc


def kernel(**inputs):
    x = np.asarray(inputs["L0_attn_x"])
    shared = dict(
        w1=np.ascontiguousarray(inputs["L0_ffn_cn_w1"], np.float32),
        w2=np.ascontiguousarray(inputs["L0_ffn_cn_w2"], np.float32),
        w3=np.ascontiguousarray(inputs["L0_ffn_cn_w3"], np.float32),
        w4=np.ascontiguousarray(inputs["L0_ffn_cn_w4"], np.float32),
        b1=np.ascontiguousarray(inputs["L0_ffn_cn_b1"], np.float32),
        b2=np.ascontiguousarray(inputs["L0_ffn_cn_b2"], np.float32),
        b3=np.ascontiguousarray(inputs["L0_ffn_cn_b3"], np.float32),
        b4=np.ascontiguousarray(inputs["L0_ffn_cn_b4"], np.float32),
        gw1=np.ascontiguousarray(inputs["L0_ffn_gc_w1"], np.float32),
        gw2=np.ascontiguousarray(inputs["L0_ffn_gc_w2"], np.float32),
        gw3=np.ascontiguousarray(inputs["L0_ffn_gc_w3"], np.float32),
        gb1=np.ascontiguousarray(inputs["L0_ffn_gc_b1"], np.float32),
        gb2=np.ascontiguousarray(inputs["L0_ffn_gc_b2"], np.float32),
        gb3=np.ascontiguousarray(inputs["L0_ffn_gc_b3"], np.float32),
        arr=np.ascontiguousarray(inputs["L0_ffn_adaptive_range"], np.float32),
        kvT=np.ascontiguousarray(
            np.concatenate([inputs["L0_attn_value_w"], inputs["L0_attn_key_w"]], 0).T,
            np.float32),
        owT=np.ascontiguousarray(np.asarray(inputs["L0_attn_output_w"]).T, np.float32),
        tsg=np.ascontiguousarray(inputs["L0_attn_time_shift_gates"], np.float32),
        lnw=np.ascontiguousarray(inputs["L0_attn_ln_w"], np.float32),
        lnb=np.ascontiguousarray(inputs["L0_attn_ln_b"], np.float32),
        td=np.ascontiguousarray(inputs["L0_attn_time_decay"], np.float32),
        tf=np.ascontiguousarray(inputs["L0_attn_time_first"], np.float32),
    )
    in_maps = []
    for i in range(8):
        xi = np.ascontiguousarray(np.swapaxes(x[2 * i:2 * i + 2], 1, 2), np.float32)
        m = dict(shared)
        m["x_fm"] = xi
        in_maps.append(m)

    nc = build_nc()
    res = run_bass_kernel_spmd(nc, in_maps, core_ids=list(range(8)))
    outs = [np.ascontiguousarray(np.swapaxes(r["y_fm"], 1, 2))
            for r in res.results]
    return np.concatenate(outs, 0).astype(np.float32)
